# version 06
# baseline (speedup 1.0000x reference)
import sys
import os
sys.path.insert(0, "/opt/trn_rl_repo")
os.environ.setdefault("TRN_ALLOW_ICI_ALL_TO_ALL", "1")
import numpy as np
import ml_dtypes
import concourse.bass as bass
import concourse.mybir as mybir
from concourse.alu_op_type import AluOpType as OP
from concourse.bass_utils import run_bass_kernel_spmd
from contextlib import ExitStack

F32 = mybir.dt.float32
BF16 = mybir.dt.bfloat16
AF = mybir.ActivationFunctionType
AX = mybir.AxisListType

B, T, D = 4, 8192, 1024
H, DK, DV, M = 4, 256, 256, 256
I = 2816
NCORES = 8
C = 256
NCH = T // C
HPC = 2
TH = T // 2
NTT = TH // 128
EPS = 1e-5
MAGIC = 6291456.0
GUT = 3
NGU = GUT * 128
GUN = I // 128
ENGS = ("sync", "gpsimd", "vector", "scalar", "tensor")
SEMS = ["ld", "tr", "gs", "cc", "vs", "ss", "ps", "st"]


class Plan:
    def __init__(self):
        self.q = {e: [] for e in ENGS}
        self.cnt = {s: 0 for s in SEMS}

    def emit(self, eng, fn):
        self.q[eng].append(fn)

    def wait(self, eng, tok):
        if tok is None:
            return
        name, val = tok
        if val <= 0:
            return
        self.emit(eng, lambda e, S, name=name, val=val: e.wait_ge(S[name], val))

    def tok(self, name):
        return (name, self.cnt[name])

    def step(self, eng, body, makes=None, amt=1, waits=(), direct=False,
             default_inc=False):
        for t in waits:
            self.wait(eng, t)
        tokout = None
        if makes is not None:
            self.cnt[makes] += amt
            tokout = (makes, self.cnt[makes])
        use_drain = (makes is not None and amt != 16 and not direct
                     and eng in ("vector", "scalar"))

        pad_drain = False and (makes is None and eng in ("vector", "scalar"))

        def fn(e, S, body=body, makes=makes, amt=amt, use_drain=use_drain,
               pad_drain=pad_drain, default_inc=default_inc):
            instr = body(e)
            if makes is not None:
                if use_drain:
                    e.drain().then_inc(S[makes], amt)
                elif default_inc:
                    instr.then_inc(S[makes])
                else:
                    instr.then_inc(S[makes], amt)
            elif pad_drain:
                e.drain()
        self.emit(eng, fn)
        return tokout

    def spacer(self, eng):
        self.emit(eng, lambda e, S: e.drain())

    def barrier(self):
        toks = [self.tok(s) for s in SEMS]
        for e in ENGS:
            for t in toks:
                self.wait(e, t)


KP_NCH = int(os.environ.get("KP_NCH", str(T // C)))
KP_NTT = int(os.environ.get("KP_NTT", str(T // 256)))
KP_NT4 = int(os.environ.get("KP_NT4", os.environ.get("KP_NTT", str(T // 256))))
KP_FS = int(os.environ.get("KP_FS", "9"))
KP_DBG = os.environ.get("KP_DBG", "0") == "1"
KP_P0 = os.environ.get("KP_P0", "1") == "1"
KP_CC = os.environ.get("KP_CC", "1") == "1"


def build_program():
    nc = bass.Bass("TRN2", target_bir_lowering=False, debug=False)

    x_in = nc.dram_tensor("x", [T, D], F32, kind="ExternalInput").ap()
    xh_in = nc.dram_tensor("xh", [TH, D], F32, kind="ExternalInput").ap()
    wqkvfT = nc.dram_tensor("wqkvfT", [D, 2048], F32, kind="ExternalInput").ap()
    woT_in = nc.dram_tensor("woT", [2048, D], F32, kind="ExternalInput").ap()
    gw_in = nc.dram_tensor("gw", [128, 16], F32, kind="ExternalInput").ap()
    msk16_in = nc.dram_tensor("msk16", [128, 32], F32, kind="ExternalInput").ap()
    wg_s = nc.dram_tensor("wg_s", [NGU, D], F32, kind="ExternalInput").ap()
    wu_s = nc.dram_tensor("wu_s", [NGU, D], F32, kind="ExternalInput").ap()
    wd_s = nc.dram_tensor("wd_s", [128, I], F32, kind="ExternalInput").ap()
    consts = nc.dram_tensor("consts", [128, 8], F32, kind="ExternalInput").ap()
    tri_in = nc.dram_tensor("tri_bf", [128, 256], BF16, kind="ExternalInput").ap()
    ones_bf_in = nc.dram_tensor("ones_bf", [128, 128], BF16, kind="ExternalInput").ap()
    ones_f_in = nc.dram_tensor("ones_f", [128, 2], F32, kind="ExternalInput").ap()
    masks_in = nc.dram_tensor("masks", [128, 512], F32, kind="ExternalInput").ap()
    out_t = nc.dram_tensor("out", [TH, D], F32, kind="ExternalOutput").ap()
    dbg_t = nc.dram_tensor("dbg", [32, 128, 2848], F32,
                           kind="ExternalOutput").ap() if KP_DBG else None

    cc_wg_in = nc.dram_tensor("cc_wg_in", [NGU, D], BF16)
    cc_wg_out = nc.dram_tensor("cc_wg_out", [NCORES * NGU, D], BF16)
    cc_wu_in = nc.dram_tensor("cc_wu_in", [NGU, D], BF16)
    cc_wu_out = nc.dram_tensor("cc_wu_out", [NCORES * NGU, D], BF16)
    cc_wd_in = nc.dram_tensor("cc_wd_in", [128, I], BF16)
    cc_wd_out = nc.dram_tensor("cc_wd_out", [NCORES * 128, I], BF16)
    wgT_d = nc.dram_tensor("wgT_d", [128, 8 * I], BF16)
    wuT_d = nc.dram_tensor("wuT_d", [128, 8 * I], BF16)
    wdT_d = nc.dram_tensor("wdT_d", [128, GUN * D], BF16)
    cc_ov_in = nc.dram_tensor("cc_ov_in", [1024, TH], BF16)
    cc_ov_out = nc.dram_tensor("cc_ov_out", [2048, TH], BF16)
    wo16_d = nc.dram_tensor("wo16_d", [128, 16 * D], BF16)
    g_d = nc.dram_tensor("g_d", [TH, D], BF16)

    P = Plan()

    with ExitStack() as stack:
        def sb(name, shape, dt=F32):
            return stack.enter_context(nc.sbuf_tensor(name, shape, dt))

        cst = sb("cst", [128, 8])
        trib = sb("trib", [128, 256], BF16)
        onesb = sb("onesb", [128, 128], BF16)
        onesf = sb("onesf", [128, 2])
        msks = sb("msks", [128, 512])
        gwsb = sb("gwsb", [128, 16])
        msk16 = sb("s_msk16", [128, 32])
        wo_tmp = sb("wo_tmp", [128, D], BF16)
        lnm = sb("lnm", [128, 16])
        rmx = sb("rmx", [128, 8])
        dec2 = sb("dec2", [128, 2])
        eGrow = sb("eGrow", [128, 256], BF16)
        qX = sb("qX", [128, 512])
        qS = sb("qS", [128, 512])
        qC = sb("qC", [128, 512])
        qD = sb("qD", [128, 512])
        qE = sb("qE", [128, 512], BF16)
        qBM = sb("qBM", [128, 32])
        qSC = sb("qSC", [128, 32])
        qRS = sb("qRS", [128, 32])
        qLN = sb("qLN", [128, 32])
        qSB = sb("qSB", [128, 32], BF16)

        ARN = 46592
        arena = sb("arena", [128, ARN])

        def carve(off, ncols, dt=F32):
            ap = arena[:, off:off + ncols]
            return ap.bitcast(BF16) if dt == BF16 else ap

        def dbg(idx, ap, tokwait):
            if not KP_DBG:
                return
            n = ap.free_size()
            P.step("gpsimd", lambda e, idx=idx, ap=ap, n=n: e.dma_start(
                out=dbg_t[idx:idx + 1, :, 0:n], in_=ap),
                makes="gs", amt=16, waits=[tokwait])

        wproj = carve(0, 8192, BF16)
        osl = carve(8192, 16384, BF16)
        xb = carve(24576, 2048)
        xnb = carve(26624, 1024, BF16)
        xnT = carve(27648, 1024, BF16)
        fsb = carve(28672, 512)
        fab = carve(29184, 512)
        efb = carve(29696, 512)
        Eb = carve(30208, 512)
        Emb = carve(30720, 512)
        sgate = carve(31232, 512)
        okb = carve(31744, 512)
        o2b = carve(32256, 512)
        lnsq = carve(31744, 1024)
        lnsq2 = carve(28672, 2048)
        eGsb = carve(33280, 256)
        qT = carve(33536, 256, BF16)
        kT = carve(33792, 256, BF16)
        vtm = carve(34048, 256, BF16)
        ktime = carve(34304, 256, BF16)
        fhi = carve(34560, 256, BF16)
        fres = carve(34816, 256, BF16)
        stl = carve(35072, 256, BF16)
        shat = carve(35328, 256, BF16)
        stT = carve(35584, 256, BF16)
        qtl = carve(35840, 256, BF16)
        qtT = carve(36096, 256, BF16)
        Abf = carve(36352, 256, BF16)
        Bbf = carve(36608, 256, BF16)
        ovq = carve(36864, 256, BF16)
        S1 = [carve(37120, 512), carve(37632, 512)]
        S2 = [carve(38144, 512), carve(38656, 512)]
        S1b = [carve(39168, 256, BF16), carve(39424, 256, BF16)]
        S2b = [carve(39680, 256, BF16), carve(39936, 256, BF16)]

        wgT = carve(0, 11264, BF16)
        wuT = carve(11264, 11264, BF16)
        wdT = carve(22528, 11264, BF16)
        fx = carve(33792, 2048)
        fxn = carve(35840, 1024)
        fxq = carve(36864, 512, BF16)
        fxqT = carve(37376, 512, BF16)
        fxqT2 = carve(44544, 512, BF16)
        fsg = carve(37888, 512)
        fh = carve(38400, 2816)
        fhq = carve(41216, 1408, BF16)
        fhqT = carve(42624, 1408, BF16)
        fg = carve(44032, 512, BF16)
        wp_in = fh
        wp_bf = fhq
        wp_tr = fhqT
        oT = carve(0, 16384, BF16)
        oT2 = carve(16384, 16384, BF16)
        wx = carve(32768, 1024)
        wgt = carve(33792, 512, BF16)
        wsf = carve(34304, 1024)
        wout = carve(35328, 1024)
        o2t = carve(36352, 2048)
        woTs = carve(38400, 8192, BF16)

        pq = stack.enter_context(nc.psum_tensor("pq", [128, 512], F32))
        pk = stack.enter_context(nc.psum_tensor("pk", [128, 512], F32))
        pv = stack.enter_context(nc.psum_tensor("pv", [128, 512], F32))
        pf = stack.enter_context(nc.psum_tensor("pf", [128, 512], F32))
        pl = stack.enter_context(nc.psum_tensor("pl", [128, 512], F32))
        pa = stack.enter_context(nc.psum_tensor("pa", [128, 512], F32))
        po = stack.enter_context(nc.psum_tensor("po", [128, 512], F32))
        pn = stack.enter_context(nc.psum_tensor("pn", [128, 512], F32))

        semh = {s: stack.enter_context(nc.semaphore(s)) for s in SEMS}
        block = stack.enter_context(nc.Block())

        def v3(ap, w=16):
            return ap.rearrange("p (b w) -> p b w", w=w)

        def emit_quant(x_ap, F, out_bf_ap, waits=()):
            nb = F // 16
            P.step("vector", lambda e: e.tensor_reduce(
                qBM[:, 0:nb], v3(x_ap), axis=AX.X, op=OP.max,
                apply_absolute_value=True),
                waits=list(waits) + [P.tok("ss")])
            tbm = P.step("vector", lambda e: e.nop(), makes="vs")
            P.step("scalar", lambda e: e.activation(
                qSC[:, 0:nb], qBM[:, 0:nb], AF.Identity,
                scale=0.16666666666666666, bias=cst[:, 2:3]), waits=[tbm])
            P.spacer("scalar")
            P.step("scalar", lambda e: e.activation(
                qLN[:, 0:nb], qSC[:, 0:nb], AF.Ln))
            P.spacer("scalar")
            tsc = P.step("scalar", lambda e: e.activation(
                qRS[:, 0:nb], qLN[:, 0:nb], AF.Exp, scale=-1.0), makes="ss")
            P.step("vector", lambda e: e.tensor_copy(qSB[:, 0:nb], qSC[:, 0:nb]),
                   waits=[tsc])
            tx = P.step("vector", lambda e: e.tensor_tensor(
                v3(qX[:, 0:F]), v3(x_ap),
                qRS[:, 0:nb].unsqueeze(2).broadcast_to([128, nb, 16]), OP.mult),
                makes="vs")
            tq = P.step("scalar", lambda e: e.activation(
                qS[:, 0:F], qX[:, 0:F], AF.Square), makes="ss", waits=[tx])
            P.step("vector", lambda e: e.tensor_scalar(
                qC[:, 0:F], qS[:, 0:F], 4.0, 1.0, OP.is_ge, OP.add), waits=[tq])
            P.step("vector", lambda e: e.tensor_scalar(
                qD[:, 0:F], qS[:, 0:F], 16.0, 1.0, OP.is_ge, OP.add))
            P.step("vector", lambda e: e.tensor_tensor(
                qS[:, 0:F], qC[:, 0:F], qD[:, 0:F], OP.mult))
            P.step("vector", lambda e: e.scalar_tensor_tensor(
                qC[:, 0:F], qS[:, 0:F], MAGIC, qX[:, 0:F], OP.mult, OP.add))
            P.step("vector", lambda e: e.scalar_tensor_tensor(
                qE[:, 0:F], qS[:, 0:F], -MAGIC, qC[:, 0:F], OP.mult, OP.add))
            P.spacer("vector")
            return P.step("vector", lambda e: e.tensor_tensor(
                v3(out_bf_ap), v3(qE[:, 0:F]),
                qSB[:, 0:nb].unsqueeze(2).broadcast_to([128, nb, 16]), OP.mult),
                makes="vs")

        def emit_ln2(x2_ap, out_aps, waits=(), scratch2=None):
            if scratch2 is None:
                scratch2 = lnsq2
            sc3 = scratch2.rearrange("p (s d2) -> p s d2", s=2)
            P.step("vector", lambda e: e.tensor_reduce(
                lnm[:, 0:2], x2_ap, axis=AX.X, op=OP.add), waits=waits)
            t0 = P.step("scalar", lambda e: e.activation(
                sc3, x2_ap, AF.Square), makes="ss", waits=waits)
            tsq = P.step("vector", lambda e: e.tensor_reduce(
                lnm[:, 2:4], sc3, axis=AX.X, op=OP.add), makes="vs", waits=[t0])
            P.step("vector", lambda e: e.tensor_scalar(
                lnm[:, 4:6], lnm[:, 0:2], 1.0 / D, None, OP.mult))
            P.step("vector", lambda e: e.tensor_scalar(
                lnm[:, 6:8], lnm[:, 2:4], 1.0 / D, None, OP.mult))
            P.spacer("vector")
            P.step("vector", lambda e: e.tensor_tensor(
                lnm[:, 8:10], lnm[:, 4:6], lnm[:, 4:6], OP.mult))
            P.spacer("vector")
            tvv = P.step("vector", lambda e: e.tensor_tensor(
                lnm[:, 10:12], lnm[:, 6:8], lnm[:, 8:10], OP.subtract), makes="vs")
            P.step("scalar", lambda e: e.activation(
                lnm[:, 10:12], lnm[:, 10:12], AF.Ln, bias=cst[:, 0:1]), waits=[tvv])
            P.spacer("scalar")
            tse = P.step("scalar", lambda e: e.activation(
                lnm[:, 12:14], lnm[:, 10:12], AF.Exp, scale=-0.5), makes="ss")
            tout = None
            for i, oa in enumerate(out_aps):
                tout = P.step("vector", lambda e, oa=oa, i=i: e.tensor_scalar(
                    oa, x2_ap[:, i, :],
                    lnm[:, 4 + i:5 + i], lnm[:, 12 + i:13 + i],
                    OP.subtract, OP.mult), makes="vs", waits=[tse] if i == 0 else ())
            return tout

        def quant_tile(src_dma_fn, F, dst_dma_fn, waits=()):
            tld = P.step("sync", src_dma_fn, makes="ld", amt=16, waits=waits)
            tq = None
            off = 0
            while off < F:
                w = min(512, F - off)
                tq = emit_quant(wp_in[:, off:off + w], w, wp_bf.bitcast(BF16) if False else wp_bf[:, off:off + w],
                                waits=[tld] if off == 0 else ())
                off += w
            return P.step("gpsimd", dst_dma_fn, makes="gs", amt=16, waits=[tq])

        for dst0, src0 in [(cst, consts), (trib, tri_in), (onesb, ones_bf_in),
                           (onesf, ones_f_in), (msks, masks_in)]:
            P.step("sync", lambda e, dst0=dst0, src0=src0: e.dma_start(
                out=dst0[:, :], in_=src0), makes="ld", amt=16)
        tcst = P.tok("ld")
        P.barrier()
        if KP_P0:
            P.step("vector", lambda e: e.memset(wp_in[:, 0:D], 1.0))
            tpad = P.step("vector", lambda e: e.nop(), makes="vs")
            prev_store = None
            first_store = None
            for (src, ccin) in [(wg_s, cc_wg_in), (wu_s, cc_wu_in)]:
                for t in range(GUT):
                    prev_store = quant_tile(
                        (lambda e, src=src, t=t: e.dma_start(
                            out=wp_in[:, 0:D], in_=src[t * 128:(t + 1) * 128, :])),
                        D,
                        (lambda e, ccin=ccin, t=t: e.dma_start(
                            out=ccin.ap()[t * 128:(t + 1) * 128, :], in_=wp_bf[:, 0:D])),
                        waits=[tpad, prev_store] if prev_store else [tpad])
                    if first_store is None:
                        first_store = prev_store
                        dbg(1, wp_in[:, 0:1024], first_store)
                        dbg(2, wp_bf[:, 0:1024], first_store)
                        dbg(3, qX[:, 0:512], first_store)
                        dbg(4, qS[:, 0:512], first_store)
                        dbg(5, qC[:, 0:512], first_store)
                        dbg(7, qE[:, 0:512], first_store)
                        dbg(8, qBM[:, 0:32], first_store)
                        dbg(9, qSC[:, 0:32], first_store)
                        dbg(10, qRS[:, 0:32], first_store)
                        prev_store = P.tok("gs")
            prev_store = quant_tile(
                lambda e: e.dma_start(out=wp_in[:, 0:I], in_=wd_s), I,
                lambda e: e.dma_start(out=cc_wd_in.ap(), in_=wp_bf[:, 0:I]),
                waits=[prev_store])
            all8 = [list(range(NCORES))]
            P.step("gpsimd", lambda e: e.collective_compute(
                "AllGather", OP.bypass, replica_groups=all8,
                ins=[cc_wg_in.ap().opt()], outs=[cc_wg_out.ap().opt()]),
                makes="cc", waits=[prev_store], direct=True)
            P.step("gpsimd", lambda e: e.collective_compute(
                "AllGather", OP.bypass, replica_groups=all8,
                ins=[cc_wu_in.ap().opt()], outs=[cc_wu_out.ap().opt()]),
            makes="cc", direct=True)
            tccw = P.step("gpsimd", lambda e: e.collective_compute(
                "AllGather", OP.bypass, replica_groups=all8,
                ins=[cc_wd_in.ap().opt()], outs=[cc_wd_out.ap().opt()]),
            makes="cc", direct=True)
            prev_tr = None
            prev_st = None
            for (ccout, wT_d, ntile) in [(cc_wg_out, wgT_d, GUN), (cc_wu_out, wuT_d, GUN)]:
                for t in range(ntile):
                    r0 = (t // GUT) * NGU + (t % GUT) * 128
                    wl = [tccw] if t == 0 and ccout is cc_wg_out else []
                    if prev_tr is not None:
                        wl.append(prev_tr)
                    tld = P.step("gpsimd", lambda e, ccout=ccout, r0=r0: e.dma_start(
                        out=wp_bf[:, 0:D], in_=ccout.ap()[r0:r0 + 128, :]),
                        makes="gs", amt=16, waits=wl)
                    wl2 = [tld] + ([prev_st] if prev_st else [])
                    prev_tr = P.step("sync", lambda e: e.dma_start_transpose(
                        wp_tr[:, 0:1024].rearrange("p (b f) -> p b f", b=8),
                        wp_bf[:, 0:D]), makes="tr", amt=16, waits=wl2)
                    prev_st = P.step("gpsimd", lambda e, wT_d=wT_d, t=t: e.dma_start(
                        out=wT_d.ap().rearrange("p (k i) -> p k i", k=8)[:, :, t * 128:(t + 1) * 128],
                        in_=wp_tr[:, 0:1024].rearrange("p (b f) -> p b f", b=8)),
                        makes="gs", amt=16, waits=[prev_tr])
            for k2 in range(8):
                tld = P.step("gpsimd", lambda e, k2=k2: e.dma_start(
                    out=wp_bf[:, 0:I], in_=cc_wd_out.ap()[k2 * 128:(k2 + 1) * 128, :]),
                    makes="gs", amt=16, waits=[prev_tr])
                prev_tr = P.step("sync", lambda e: e.dma_start_transpose(
                    wp_tr[:, 0:GUN * 128].rearrange("p (b f) -> p b f", b=GUN),
                    wp_bf[:, 0:I]), makes="tr", amt=16, waits=[tld, prev_st])
                prev_st = P.step("gpsimd", lambda e, k2=k2: e.dma_start(
                    out=wdT_d.ap().rearrange("p (j d2) -> p j d2", j=GUN)[:, :, k2 * 128:(k2 + 1) * 128],
                    in_=wp_tr[:, 0:GUN * 128].rearrange("p (b f) -> p b f", b=GUN)),
                    makes="gs", amt=16, waits=[prev_tr])

        P.step("sync", lambda e: e.dma_start(out=gwsb[:, :], in_=gw_in),
               makes="ld", amt=16)
        P.step("sync", lambda e: e.dma_start(out=msk16[:, :], in_=msk16_in),
               makes="ld", amt=16)
        tgw = P.tok("ld")
        prev = None
        for k in range(16):
            tldw = P.step("gpsimd", lambda e, k=k: e.dma_start(
                out=wo_tmp[:, :], in_=woT_in[k * 128:(k + 1) * 128, :]),
                makes="gs", amt=16, waits=[prev] if prev else [])
            tmul = P.step("vector", lambda e, k=k: e.tensor_scalar(
                wo_tmp[:, :], wo_tmp[:, :], gwsb[:, k:k + 1], None,
                OP.mult), makes="vs", waits=[tldw, tgw])
            prev = P.step("gpsimd", lambda e, k=k: e.dma_start(
                out=wo16_d.ap()[:, k * D:(k + 1) * D], in_=wo_tmp[:, :]),
                makes="gs", amt=16, waits=[tmul])

        P.barrier()

        P.step("gpsimd", lambda e: e.dma_start(
            out=wproj.rearrange("p (k c) -> p k c", k=8),
            in_=wqkvfT.rearrange("(k p) c -> p k c", p=128)), makes="gs", amt=16)
        for hh in range(HPC):
            P.step("vector", lambda e, hh=hh: e.memset(S1[hh][:, :], 0.0))
            P.step("vector", lambda e, hh=hh: e.memset(S2[hh][:, :], 0.0))
            P.step("vector", lambda e, hh=hh: e.memset(S1b[hh][:, :], 0.0))
            P.step("vector", lambda e, hh=hh: e.memset(S2b[hh][:, :], 0.0))
        P.barrier()

        def chunk_head(c, h):
            wp3 = wproj.rearrange("p (k c2) -> p k c2", k=8)
            xnT3 = xnT.rearrange("p (k t) -> p k t", k=8)
            colb = h * 256

            def proj_fm(pdst, mat):
                tk = None
                for mt in range(2):
                    for k in range(8):
                        col = mat * 512 + colb + mt * 128
                        tk = P.step("tensor", lambda e, k=k, col=col, mt=mt, pdst=pdst: e.matmul(
                            pdst[:, mt * 256:(mt + 1) * 256],
                            wp3[:, k, col:col + 128], xnT3[:, k, :],
                            start=(k == 0), stop=(k == 7)),
                            makes="ps" if k == 7 else None)
                return tk

            def proj_tm(pdst, mat):
                tk = None
                for tt in range(2):
                    for k in range(8):
                        col = mat * 512 + colb
                        tk = P.step("tensor", lambda e, k=k, col=col, tt=tt, pdst=pdst: e.matmul(
                            pdst[:, tt * 256:(tt + 1) * 256],
                            xnT3[:, k, tt * 128:(tt + 1) * 128],
                            wp3[:, k, col:col + 256],
                            start=(k == 0), stop=(k == 7)),
                            makes="ps" if k == 7 else None)
                return tk

            tq_ps = proj_fm(pq, 0)
            tk_ps = proj_fm(pk, 1)
            tv_ps = proj_tm(pv, 2)
            tf_ps = proj_tm(pf, 3)

            tqe = P.step("vector", lambda e: e.tensor_copy(ovq[:, :], pq[:, :]),
                         makes="vs", waits=[tq_ps])
            tqs = P.step("scalar", lambda e: e.activation(qT[:, :], ovq[:, :], AF.Silu),
                         makes="ss", waits=[tqe])
            tke = P.step("vector", lambda e: e.tensor_copy(ovq[:, :], pk[:, :]),
                         makes="vs", waits=[tk_ps, tqs])
            tks = P.step("scalar", lambda e: e.activation(kT[:, :], ovq[:, :], AF.Silu),
                         makes="ss", waits=[tke])
            tve = P.step("vector", lambda e: e.tensor_copy(ovq[:, :], pv[:, :]),
                         makes="vs", waits=[tv_ps, tks])
            tvs = P.step("scalar", lambda e: e.activation(vtm[:, :], ovq[:, :], AF.Silu),
                         makes="ss", waits=[tve])
            tf0 = P.step("scalar", lambda e: e.activation(fsb[:, :], pf[:, :], AF.Identity),
                         makes="ss", waits=[tf_ps])
            P.step("scalar", lambda e: e.activation(fab[:, :], fsb[:, :], AF.Abs))
            P.step("scalar", lambda e: e.activation(fab[:, :], fab[:, :], AF.Exp, scale=-1.0))
            tfl = P.step("scalar", lambda e: e.activation(fab[:, :], fab[:, :], AF.Ln,
                                                          bias=cst[:, 1:2]), makes="ss")
            tfr = P.step("vector", lambda e: e.scalar_tensor_tensor(
                fsb[:, :], fsb[:, :], 0.0, fab[:, :], OP.min, OP.subtract),
                makes="vs", waits=[tfl])
            P.step("vector", lambda e: e.tensor_copy(fhi[:, :], fsb[:, :]))
            P.spacer("vector")
            tfs = P.step("vector", lambda e: e.tensor_tensor(
                fres[:, :], fsb[:, :], fhi[:, :], OP.subtract), makes="vs")
            tef = P.step("scalar", lambda e: e.activation(efb[:, :], fsb[:, :], AF.Exp,
                                                          scale=0.125), makes="ss", waits=[tfr])
            tsg = P.step("vector", lambda e: e.tensor_scalar(
                sgate[:, :], efb[:, :], -1.0, 1.0, OP.mult, OP.add),
                makes="vs", waits=[tef])
            TRI = trib[:, 0:128]
            FUL = trib[:, 128:256]
            tlam = None
            for tt, seq in [(0, [(TRI, fhi, 0), (TRI, fres, 0)]),
                            (1, [(FUL, fhi, 0), (FUL, fres, 0), (TRI, fhi, 1), (TRI, fres, 1)])]:
                n = len(seq)
                for i2, (lt, rt, us) in enumerate(seq):
                    tlam = P.step("tensor", lambda e, lt=lt, rt=rt, us=us, tt=tt, i2=i2, n=n: e.matmul(
                        pl[:, tt * 256:(tt + 1) * 256], lt, rt[:, us * 256:(us + 1) * 256],
                        start=(i2 == 0), stop=(i2 == n - 1)),
                        makes="ps" if i2 == n - 1 else None,
                        waits=[tfs] if (tt == 0 and i2 == 0) else [])
            P.step("scalar", lambda e: e.activation(Eb[:, :], pl[:, :], AF.Exp),
                   waits=[tlam])
            tEm = P.step("scalar", lambda e: e.activation(Emb[:, :], pl[:, :], AF.Exp,
                                                          scale=-1.0), makes="ss")
            lseq = [(fhi, 0), (fres, 0), (fhi, 1), (fres, 1)]
            tlr = None
            for i2, (rt, us) in enumerate(lseq):
                tlr = P.step("tensor", lambda e, rt=rt, us=us, i2=i2: e.matmul(
                    pa[0:1, 0:256], onesb[:, 0:1], rt[:, us * 256:(us + 1) * 256],
                    start=(i2 == 0), stop=(i2 == 3)),
                    makes="ps" if i2 == 3 else None,
                    waits=[tfs] if i2 == 0 else [])
            tegf = P.step("scalar", lambda e: e.activation(
                eGsb[0:1, 0:256], pa[0:1, 0:256], AF.Exp, scale=0.125),
                makes="ss", waits=[tlr])
            tegb = P.step("vector", lambda e: e.tensor_copy(eGrow[0:1, 0:256],
                                                            eGsb[0:1, 0:256]),
                          makes="vs", waits=[tegf])
            tdc = None
            for ms in range(2):
                for i2, (rt, us) in enumerate(lseq):
                    tdc = P.step("tensor", lambda e, ms=ms, i2=i2, rt=rt, us=us: e.matmul(
                        pn[:, ms:ms + 1],
                        rt[:, us * 256 + ms * 128: us * 256 + ms * 128 + 128],
                        onesb[:, 0:1], start=(i2 == 0), stop=(i2 == 3)),
                        makes="ps" if i2 == 3 else None,
                        waits=[tfs] if (ms == 0 and i2 == 0) else [])
            tdc2 = P.step("scalar", lambda e: e.activation(dec2[:, 0:2], pn[:, 0:2],
                                                           AF.Exp, scale=0.125),
                          makes="ss", waits=[tdc])
            tge = P.step("tensor", lambda e: e.matmul(
                pn[:, 256:512], onesb[0:1, 0:128], eGrow[0:1, 0:256],
                start=True, stop=True), makes="ps", waits=[tegb, tdc2])
            teg = P.step("vector", lambda e: e.tensor_copy(eGsb[:, :], pn[:, 256:512]),
                         makes="vs", waits=[tge])
            tst = P.step("vector", lambda e: e.tensor_tensor(
                stl[:, :], sgate[:, :], Emb[:, :], OP.mult),
                makes="vs", waits=[tsg, tEm])
            P.spacer("vector")
            tsh = P.step("vector", lambda e: e.tensor_tensor(
                stl[:, :].rearrange("p (s m2) -> p s m2", s=2) if False else shat[:, :].rearrange("p (s m2) -> p s m2", s=2),
                stl[:, :].rearrange("p (s m2) -> p s m2", s=2),
                eGsb[:, :].unsqueeze(1).broadcast_to([128, 2, 256]), OP.mult),
                makes="vs", waits=[teg])
            tkt = None
            for j in range(2):
                tkt = P.step("sync", lambda e, j=j: e.dma_start_transpose(
                    ktime[:, :].rearrange("p (s d2) -> p s d2", s=2)[:, :, j * 128:(j + 1) * 128],
                    kT[:, j * 256:(j + 1) * 256]), makes="tr", amt=16,
                    waits=[tks] if j == 0 else [])
            tstT = None
            for ts2 in range(2):
                tstT = P.step("sync", lambda e, ts2=ts2: e.dma_start_transpose(
                    stT[:, :].rearrange("p (s t2) -> p s t2", s=2)[:, :, ts2 * 128:(ts2 + 1) * 128],
                    stl[:, ts2 * 256:(ts2 + 1) * 256]), makes="tr", amt=16,
                    waits=[tst] if ts2 == 0 else [])
            taq = None
            for ut in range(2):
                for j in range(2):
                    taq = P.step("tensor", lambda e, ut=ut, j=j: e.matmul(
                        pa[:, ut * 256:(ut + 1) * 256],
                        kT[:, j * 256 + ut * 128: j * 256 + ut * 128 + 128],
                        qT[:, j * 256:(j + 1) * 256],
                        start=(j == 0), stop=(j == 1)),
                        makes="ps" if j == 1 else None,
                        waits=[tqs, tks, tegf] if (ut == 0 and j == 0) else [])
            tA = None
            for ut in range(2):
                tA = P.step("vector", lambda e, ut=ut: e.tensor_tensor(
                    Abf[:, ut * 256:(ut + 1) * 256], msks[:, ut * 256:(ut + 1) * 256],
                    pa[:, ut * 256:(ut + 1) * 256], OP.mult),
                    makes="vs", waits=[taq] if ut == 0 else [])
            tokp = None
            for tt in range(2):
                mm = [("intra", j) for j in range(2)] + [("inter", j) for j in range(2)]
                for i2, (kind, j) in enumerate(mm):
                    if kind == "intra":
                        def body(e, j=j, tt=tt, i2=i2, n=len(mm)):
                            return e.matmul(
                                po[:, tt * 256:(tt + 1) * 256],
                                Abf[:, j * 256 + tt * 128: j * 256 + tt * 128 + 128],
                                stl[:, j * 256:(j + 1) * 256],
                                start=(i2 == 0), stop=(i2 == n - 1))
                    else:
                        def body(e, j=j, tt=tt, i2=i2, n=len(mm), h=h):
                            return e.matmul(
                                po[:, tt * 256:(tt + 1) * 256],
                                qT[:, j * 256 + tt * 128: j * 256 + tt * 128 + 128],
                                S1b[h][:, j * 256:(j + 1) * 256],
                                start=(i2 == 0), stop=(i2 == n - 1))
                    tokp = P.step("tensor", body,
                                  makes="ps" if i2 == len(mm) - 1 else None,
                                  waits=[tA, tst] if (tt == 0 and i2 == 0) else [])
            tokE = P.step("vector", lambda e: e.tensor_tensor(
                okb[:, :], Eb[:, :], po[:, :], OP.mult), makes="vs", waits=[tokp])
            tmx = P.step("vector", lambda e: e.tensor_reduce(
                rmx[:, 0:2], okb[:, :].rearrange("p (s m2) -> p s m2", s=2),
                axis=AX.X, op=OP.max, negate=True), makes="vs", waits=[tokE])
            tex0 = None
            for tt in range(2):
                tex0 = P.step("scalar", lambda e, tt=tt: e.activation(
                    o2b[:, tt * 256:(tt + 1) * 256], okb[:, tt * 256:(tt + 1) * 256],
                    AF.Exp, bias=rmx[:, tt:tt + 1]),
                    makes="ss", waits=[tmx] if tt == 0 else ())
            tex = P.step("vector", lambda e: e.tensor_reduce(
                rmx[:, 2:4], o2b[:, :].rearrange("p (s m2) -> p s m2", s=2),
                axis=AX.X, op=OP.add), makes="vs", waits=[tex0])
            P.step("scalar", lambda e: e.activation(
                rmx[:, 6:8], rmx[:, 2:4], AF.Ln), waits=[tex])
            P.spacer("scalar")
            trc = P.step("scalar", lambda e: e.activation(
                rmx[:, 4:6], rmx[:, 6:8], AF.Exp, scale=-1.0), makes="ss")
            tsm = None
            for tt in range(2):
                tsm = P.step("vector", lambda e, tt=tt: e.scalar_tensor_tensor(
                    qtl[:, tt * 256:(tt + 1) * 256], o2b[:, tt * 256:(tt + 1) * 256],
                    rmx[:, 4 + tt:5 + tt], Eb[:, tt * 256:(tt + 1) * 256],
                    OP.mult, OP.mult), makes="vs", waits=[trc] if tt == 0 else ())
            tqtT = None
            for ts2 in range(2):
                tqtT = P.step("sync", lambda e, ts2=ts2: e.dma_start_transpose(
                    qtT[:, :].rearrange("p (s t2) -> p s t2", s=2)[:, :, ts2 * 128:(ts2 + 1) * 128],
                    qtl[:, ts2 * 256:(ts2 + 1) * 256]), makes="tr", amt=16,
                    waits=[tsm] if ts2 == 0 else [])
            tbq = None
            for ut in range(2):
                for j in range(2):
                    tbq = P.step("tensor", lambda e, ut=ut, j=j: e.matmul(
                        pa[:, ut * 256:(ut + 1) * 256],
                        stT[:, j * 256 + ut * 128: j * 256 + ut * 128 + 128],
                        qtT[:, j * 256:(j + 1) * 256],
                        start=(j == 0), stop=(j == 1)),
                        makes="ps" if j == 1 else None,
                        waits=[tstT, tqtT, tA] if (ut == 0 and j == 0) else [])
            tB = None
            for ut in range(2):
                tB = P.step("vector", lambda e, ut=ut: e.tensor_tensor(
                    Bbf[:, ut * 256:(ut + 1) * 256], msks[:, ut * 256:(ut + 1) * 256],
                    pa[:, ut * 256:(ut + 1) * 256], OP.mult),
                    makes="vs", waits=[tbq] if ut == 0 else [])
            tov = None
            for vs2 in range(2):
                mm = [("intra", j) for j in range(2)] + [("inter", j) for j in range(2)]
                for i2, (kind, j) in enumerate(mm):
                    if kind == "intra":
                        def body(e, j=j, vs2=vs2, i2=i2, n=len(mm)):
                            return e.matmul(
                                po[:, vs2 * 256:(vs2 + 1) * 256],
                                vtm[:, j * 256 + vs2 * 128: j * 256 + vs2 * 128 + 128],
                                Bbf[:, j * 256:(j + 1) * 256],
                                start=(i2 == 0), stop=(i2 == n - 1))
                    else:
                        def body(e, j=j, vs2=vs2, i2=i2, n=len(mm), h=h):
                            return e.matmul(
                                po[:, vs2 * 256:(vs2 + 1) * 256],
                                S2b[h][:, j * 256 + vs2 * 128: j * 256 + vs2 * 128 + 128],
                                qtT[:, j * 256:(j + 1) * 256],
                                start=(i2 == 0), stop=(i2 == n - 1))
                    tov = P.step("tensor", body,
                                 makes="ps" if i2 == len(mm) - 1 else None,
                                 waits=[tB, tvs, tokE] if (vs2 == 0 and i2 == 0) else [])
            osl3 = osl.rearrange("p (f t) -> p f t", f=4)
            sl = osl3[:, 2 * h:2 * h + 2, c * C:(c + 1) * C]
            toe = P.step("vector", lambda e: e.tensor_copy(ovq[:, :], po[:, :]),
                         makes="vs", waits=[tov])
            tos = P.step("scalar", lambda e: e.activation(sl, ovq[:, :], AF.Silu),
                         makes="ss", waits=[toe])
            tn1 = None
            for ds in range(2):
                for j in range(2):
                    tn1 = P.step("tensor", lambda e, ds=ds, j=j: e.matmul(
                        pq[:, ds * 256:(ds + 1) * 256],
                        ktime[:, j * 256 + ds * 128: j * 256 + ds * 128 + 128],
                        shat[:, j * 256:(j + 1) * 256],
                        start=(j == 0), stop=(j == 1)),
                        makes="ps" if j == 1 else None,
                        waits=[tkt, tsh, tqe, tokp] if (ds == 0 and j == 0) else [])
            tn2 = None
            for ms in range(2):
                for j in range(2):
                    tn2 = P.step("tensor", lambda e, ms=ms, j=j: e.matmul(
                        pk[:, ms * 256:(ms + 1) * 256],
                        shat[:, j * 256 + ms * 128: j * 256 + ms * 128 + 128],
                        vtm[:, j * 256:(j + 1) * 256],
                        start=(j == 0), stop=(j == 1)),
                        makes="ps" if j == 1 else None,
                        waits=[tke, tov] if (ms == 0 and j == 0) else [])
            tS1m = P.step("vector", lambda e, h=h: e.tensor_tensor(
                fab[:, :].rearrange("p (s m2) -> p s m2", s=2),
                S1[h][:, :].rearrange("p (s m2) -> p s m2", s=2),
                eGsb[:, :].unsqueeze(1).broadcast_to([128, 2, 256]), OP.mult),
                makes="vs", waits=[tn1, teg])
            tS1 = P.step("vector", lambda e, h=h: e.tensor_tensor(
                S1[h][:, :], fab[:, :], pq[:, :], OP.add), makes="vs")
            tS2 = None
            for ms in range(2):
                tS2 = P.step("vector", lambda e, h=h, ms=ms: e.scalar_tensor_tensor(
                    S2[h][:, ms * 256:(ms + 1) * 256], S2[h][:, ms * 256:(ms + 1) * 256],
                    dec2[:, ms:ms + 1], pk[:, ms * 256:(ms + 1) * 256],
                    OP.mult, OP.add), makes="vs", waits=[tn2] if ms == 0 else [])
            P.step("vector", lambda e, h=h: e.tensor_copy(S1b[h][:, :], S1[h][:, :]),
                   waits=[tS1])
            P.step("vector", lambda e, h=h: e.tensor_copy(S2b[h][:, :], S2[h][:, :]),
                   waits=[tS2])
            tend = P.step("vector", lambda e: e.nop(), makes="vs")
            if c == 0 and h == 0:
                dbg(1, qT[:, :], tend)
                dbg(2, kT[:, :], tend)
                dbg(3, vtm[:, :], tend)
                dbg(4, fsb[:, :], tend)
                dbg(5, Eb[:, :], tend)
                dbg(6, Emb[:, :], tend)
                dbg(7, okb[:, :], tend)
                dbg(8, qtl[:, :], tend)
                dbg(9, Abf[:, :], tend)
                dbg(10, Bbf[:, :], tend)
                dbg(11, stl[:, :], tend)
                dbg(12, shat[:, :], tend)
                dbg(13, eGsb[:, :], tend)
                dbg(14, S1[0][:, :], tend)
                dbg(15, S2[0][:, :], tend)
                dbg(30, rmx[:, 0:8], tend)
                dbg(31, o2b[:, :], tend)

        for c in range(KP_NCH):
            P.step("sync", lambda e, c=c: e.dma_start(
                out=xb[:, 0:D], in_=x_in[c * C:c * C + 128, :]),
                makes="ld", amt=16)
            tx = P.step("sync", lambda e, c=c: e.dma_start(
                out=xb[:, D:2 * D], in_=x_in[c * C + 128:(c + 1) * C, :]),
                makes="ld", amt=16)
            tln = emit_ln2(xb.rearrange("p (s d2) -> p s d2", s=2),
                           [xnb[:, 0:D], xnb[:, D:2 * D]], waits=[tx])
            ttr = None
            for tt in range(2):
                ttr = P.step("sync", lambda e, tt=tt: e.dma_start_transpose(
                    xnT.rearrange("p (k t) -> p k t", k=8)[:, :, tt * 128:(tt + 1) * 128],
                    xnb[:, tt * D:(tt + 1) * D]), makes="tr", amt=16,
                    waits=[tln] if tt == 0 else [])
            if c == 0:
                dbg(0, xnb, P.tok("vs"))
                dbg(29, lnm[:, 0:16], P.tok("vs"))
                dbg(6, xb[:, 0:2048], P.tok("vs"))
            P.barrier()
            for h in range(HPC):
                chunk_head(c, h)
                P.barrier()
            if c == 0:
                dbg(16, osl.rearrange("p (f t) -> p f t", f=4)[:, :, 0:256],
                    P.tok("ss"))

        osl3x = osl.rearrange("p (f t) -> p f t", f=4)
        for jh in range(2):
            P.step("gpsimd", lambda e, jh=jh: e.dma_start(
                out=cc_ov_in.ap()[jh * 512:(jh + 1) * 512, :].rearrange(
                    "(f p) t -> p f t", p=128),
                in_=osl3x[:, :, jh * TH:(jh + 1) * TH]), makes="gs", amt=16)
        tst2 = P.tok("gs")
        pairs = [[0, 1], [2, 3], [4, 5], [6, 7]]
        if KP_CC:
            P.step("gpsimd", lambda e: e.collective_compute(
                "AllGather", OP.bypass, replica_groups=pairs,
                ins=[cc_ov_in.ap().opt()], outs=[cc_ov_out.ap().opt()]),
                makes="cc", waits=[tst2], direct=True)
        P.barrier()

        P.step("sync", lambda e: e.dma_start(out=wgT[:, :], in_=wgT_d.ap()),
               makes="ld", amt=16)
        P.step("sync", lambda e: e.dma_start(out=wuT[:, :], in_=wuT_d.ap()),
               makes="ld", amt=16)
        P.step("sync", lambda e: e.dma_start(out=wdT[:, :], in_=wdT_d.ap()),
               makes="ld", amt=16)
        P.barrier()
        wgT3 = wgT.rearrange("p (k i2) -> p k i2", k=8)
        wuT3 = wuT.rearrange("p (k i2) -> p k i2", k=8)
        wdT3 = wdT.rearrange("p (j d2) -> p j d2", j=GUN)
        fxqT3s = [fxqT.rearrange("p (k t) -> p k t", k=8),
                  fxqT2.rearrange("p (k t) -> p k t", k=8)]
        fhqT3 = fhqT.rearrange("p (j t) -> p j t", j=GUN)
        SECS = [(si * 512, min(512, I - si * 512)) for si in range(6)]

        fp = {"tln": None, "thq": None, "tqq": {}, "ttr": {}, "pe_gu": {},
              "prev_ge": None, "prev_he": None, "prev_th5": None,
              "tdn1": None, "tss2": [None, None], "tgd": None}

        def emit_xprep(tt):
            P.step("sync", lambda e, tt=tt: e.dma_start(
                out=fx[:, 0:D], in_=xh_in[tt * 128:(tt + 1) * 128, :]),
                makes="ld", amt=16,
                waits=[fp["tln"]] if fp["tln"] else [])
            tx = P.step("sync", lambda e, tt=tt: e.dma_start(
                out=fx[:, D:2 * D], in_=xh_in[tt * 128:(tt + 1) * 128, :]),
                makes="ld", amt=16)
            lw = [tx] + ([fp["thq"]] if fp["thq"] else [])
            tln = emit_ln2(fx.rearrange("p (s d2) -> p s d2", s=2),
                           [fxn[:, :]], waits=lw, scratch2=fh[:, 0:2 * D])
            tqq = emit_quant(fxn[:, 0:512], 512, fxq[:, 0:512], waits=[tln])
            tqq = emit_quant(fxn[:, 512:1024], 512, fxq[:, 512:1024], waits=[tqq])
            tw = [tqq] + ([fp["pe_gu"][tt - 2]] if (tt - 2) in fp["pe_gu"] else [])
            ttr = P.step("sync", lambda e, tt=tt: e.dma_start_transpose(
                fxqT3s[tt % 2], fxq[:, :]), makes="tr", amt=16, waits=tw)
            fp["tln"], fp["tqq"][tt], fp["ttr"][tt] = tln, tqq, ttr

        emit_xprep(0)
        for tt in range(KP_NTT):
            fxqT3 = fxqT3s[tt % 2]
            thq = None
            for (off, w) in SECS:
                tg = None
                for k in range(8):
                    if k == 0:
                        w0 = [fp["ttr"][tt]] if off == 0 else [fp["prev_ge"]]
                        if off == 0 and fp["prev_ge"] is not None:
                            w0.append(fp["prev_ge"])
                    else:
                        w0 = []
                    tg = P.step("tensor", lambda e, k=k, off=off, w=w, fxqT3=fxqT3: e.matmul(
                        pq[:, 0:w], fxqT3[:, k, :], wgT3[:, k, off:off + w],
                        start=(k == 0), stop=(k == 7)),
                        makes="ps" if k == 7 else None, waits=w0)
                tu = None
                for k in range(8):
                    w0 = [fp["prev_he"]] if (k == 0 and fp["prev_he"] is not None) else []
                    tu = P.step("tensor", lambda e, k=k, off=off, w=w, fxqT3=fxqT3: e.matmul(
                        pk[:, 0:w], fxqT3[:, k, :], wuT3[:, k, off:off + w],
                        start=(k == 0), stop=(k == 7)),
                        makes="ps" if k == 7 else None, waits=w0)
                sw = [tg] + ([fp["prev_he"]] if fp["prev_he"] is not None else [])
                tsg2 = P.step("scalar", lambda e, w=w: e.activation(
                    fsg[:, 0:w], pq[:, 0:w], AF.Silu), makes="ss", waits=sw)
                th = P.step("vector", lambda e, off=off, w=w: e.tensor_tensor(
                    fh[:, off:off + w], fsg[:, 0:w], pk[:, 0:w], OP.mult),
                    makes="vs", waits=[tsg2, tu])
                fp["prev_ge"], fp["prev_he"] = tsg2, th
                qw = [th] + ([fp["tdn1"]] if (off == 0 and fp["tdn1"]) else [])
                thq = emit_quant(fh[:, off:off + w], w,
                                 fhq[:, off:off + w], waits=qw)
            fp["thq"] = thq
            fp["pe_gu"][tt] = tu
            fp["prev_th5"] = th
            tw2 = [thq] + ([fp["tdn1"]] if fp["tdn1"] else [])
            ttr2 = P.step("sync", lambda e: e.dma_start_transpose(
                fhqT3, fhq[:, :]), makes="tr", amt=16, waits=tw2)
            if tt + 1 < KP_NTT:
                emit_xprep(tt + 1)
            tdn = []
            for half2 in range(2):
                td = None
                for j in range(GUN):
                    if j == 0:
                        w0 = [ttr2] + ([fp["tss2"][half2]] if fp["tss2"][half2] else [])
                    else:
                        w0 = []
                    td = P.step("tensor", lambda e, j=j, half2=half2: e.matmul(
                        (pv if half2 == 0 else pf)[:, 0:512],
                        fhqT3[:, j, :], wdT3[:, j, half2 * 512:(half2 + 1) * 512],
                        start=(j == 0), stop=(j == GUN - 1)),
                        makes="ps" if j == GUN - 1 else None,
                        waits=w0)
                tdn.append(td)
            fp["tdn1"] = tdn[1]
            tgo = None
            for half2 in range(2):
                sw2 = [tdn[half2]] + ([fp["prev_th5"]] if half2 == 0 else [tgo])
                tss2 = P.step("scalar", lambda e, half2=half2: e.activation(
                    fsg[:, 0:512], (pv if half2 == 0 else pf)[:, 0:512], AF.Silu),
                    makes="ss", waits=sw2)
                fp["tss2"][half2] = tss2
                gw2 = [tss2] + ([fp["tgd"]] if (half2 == 0 and fp["tgd"]) else [])
                tgo = P.step("vector", lambda e, half2=half2: e.tensor_copy(
                    fg[:, half2 * 512:(half2 + 1) * 512], fsg[:, 0:512]),
                    makes="vs", waits=gw2)
            tgd = P.step("gpsimd", lambda e, tt=tt: e.dma_start(
                out=g_d.ap()[tt * 128:(tt + 1) * 128, :], in_=fg[:, :]),
                makes="gs", amt=16, waits=[tgo])
            fp["tgd"] = tgd
            fp["prev_ge"] = tgo
        P.barrier()

        P.step("sync", lambda e: e.dma_start(
            out=oT.rearrange("p (k t) -> p k t", k=8),
            in_=cc_ov_out.ap()[0:1024, :].rearrange("(k p) t -> p k t", p=128)),
            makes="ld", amt=16)
        P.step("sync", lambda e: e.dma_start(
            out=oT2.rearrange("p (k t) -> p k t", k=8),
            in_=cc_ov_out.ap()[1024:2048, :].rearrange("(k p) t -> p k t", p=128)),
            makes="ld", amt=16)
        P.step("sync", lambda e: e.dma_start(
            out=woTs.rearrange("p (k d2) -> p k d2", k=16)[:, 0:8, :],
            in_=wo16_d.ap().rearrange("p (k d2) -> p k d2", k=16)[:, 0:8, :]),
            makes="ld", amt=16)
        P.step("sync", lambda e: e.dma_start(
            out=woTs.rearrange("p (k d2) -> p k d2", k=16)[:, 8:16, :],
            in_=wo16_d.ap().rearrange("p (k d2) -> p k d2", k=16)[:, 8:16, :]),
            makes="ld", amt=16)
        P.barrier()
        oT3a = oT.rearrange("p (k t) -> p k t", k=8)
        oT3b = oT2.rearrange("p (k t) -> p k t", k=8)
        woTs3 = woTs.rearrange("p (k d2) -> p k d2", k=16)
        o2t3 = o2t.rearrange("p (k t) -> p k t", k=16)
        laststore = None
        for tt in range(KP_NT4):
            txl = P.step("sync", lambda e, tt=tt: e.dma_start(
                out=wx[:, :], in_=xh_in[tt * 128:(tt + 1) * 128, :]),
                makes="ld", amt=16)
            tgl = P.step("gpsimd", lambda e, tt=tt: e.dma_start(
                out=wgt[:, :], in_=g_d.ap()[tt * 128:(tt + 1) * 128, :]),
                makes="gs", amt=16)
            tq2 = P.step("scalar", lambda e, tt=tt: e.activation(
                o2t3[:, 0:8, :], oT3a[:, :, tt * 128:(tt + 1) * 128], AF.Square),
                makes="ss")
            tq3 = P.step("scalar", lambda e, tt=tt: e.activation(
                o2t3[:, 8:16, :], oT3b[:, :, tt * 128:(tt + 1) * 128], AF.Square),
                makes="ss")
            tssq = None
            for k in range(16):
                tssq = P.step("tensor", lambda e, k=k: e.matmul(
                    pv[:, 0:2], o2t3[:, k, :], msk16[:, 2 * k:2 * k + 2],
                    start=(k == 0), stop=(k == 15)),
                    makes="ps" if k == 15 else None,
                    waits=[tq2, tq3] if k == 0 else [])
            tr1 = P.step("vector", lambda e: e.tensor_scalar(
                rmx[:, 0:2], pv[:, 0:2], 1.0 / (H * DV), None, OP.mult),
                makes="vs", waits=[tssq])
            P.step("scalar", lambda e: e.activation(
                rmx[:, 2:4], rmx[:, 0:2], AF.Ln, bias=cst[:, 0:1]), waits=[tr1])
            P.spacer("scalar")
            trt = P.step("scalar", lambda e: e.activation(
                rmx[:, 0:2], rmx[:, 2:4], AF.Exp, scale=-0.5), makes="ss")
            tmm = []
            for half2 in range(2):
                tm2 = None
                for k in range(16):
                    src = oT3a if k < 8 else oT3b
                    tm2 = P.step("tensor", lambda e, k=k, half2=half2, tt=tt, src=src: e.matmul(
                        (pq if half2 == 0 else pk)[:, 0:512],
                        src[:, k % 8, tt * 128:(tt + 1) * 128],
                        woTs3[:, k, half2 * 512:(half2 + 1) * 512],
                        start=(k == 0), stop=(k == 15)),
                        makes="ps" if k == 15 else None)
                tmm.append(tm2)
            tsf = None
            for half2 in range(2):
                tsf = P.step("vector", lambda e, half2=half2: e.tensor_scalar(
                    wsf[:, half2 * 512:(half2 + 1) * 512],
                    (pq if half2 == 0 else pk)[:, 0:512],
                    rmx[:, 0:1], None, OP.mult),
                    makes="vs", waits=[tmm[half2], trt] if half2 == 0 else [tmm[half2]])
            tw1 = P.step("vector", lambda e: e.tensor_tensor(
                wout[:, :], wsf[:, :], wgt[:, :], OP.mult),
                makes="vs", waits=[tsf, tgl])
            tw2 = P.step("vector", lambda e: e.tensor_tensor(
                wout[:, :], wout[:, :], wx[:, :], OP.add),
                makes="vs", waits=[txl])
            laststore = P.step("sync", lambda e, tt=tt: e.dma_start(
                out=out_t[tt * 128:(tt + 1) * 128, :], in_=wout[:, :]),
                makes="st", amt=16, waits=[tw2])
            if tt == 0:
                dbg(24, wsf[:, :], laststore)
                dbg(25, rmx[:, 0:8], laststore)
                dbg(26, oT.rearrange("p (k t) -> p k t", k=8)[:, :, 0:128], laststore)
                dbg(27, oT2.rearrange("p (k t) -> p k t", k=8)[:, :, 0:128], laststore)
                dbg(28, wgt[:, :], laststore)
            P.barrier()
        if laststore is not None:
            P.wait("sync", laststore)
            P.wait("gpsimd", P.tok("st"))

        @block.sync
        def _(sync):
            for fn in P.q["sync"]:
                fn(sync, semh)

        @block.gpsimd
        def _(g):
            for fn in P.q["gpsimd"]:
                fn(g, semh)

        @block.vector
        def _(v):
            for fn in P.q["vector"]:
                fn(v, semh)

        @block.scalar
        def _(s):
            for fn in P.q["scalar"]:
                fn(s, semh)

        @block.tensor
        def _(t):
            for fn in P.q["tensor"]:
                fn(t, semh)

    return nc


def _tri_mask(p, f):
    return (np.arange(p)[:, None] <= np.arange(f)[None, :]).astype(np.float32)


def kernel(L0_attn_hidden_states, L0_attn_Wq, L0_attn_Wk, L0_attn_Wv, L0_attn_Wf,
           L0_attn_g_w, L0_attn_Wo, L0_ffn_w_gate, L0_ffn_w_up, L0_ffn_w_down):
    x = np.ascontiguousarray(np.asarray(L0_attn_hidden_states, dtype=np.float32))
    Wq = np.asarray(L0_attn_Wq, np.float32)
    Wk = np.asarray(L0_attn_Wk, np.float32)
    Wv = np.asarray(L0_attn_Wv, np.float32)
    Wf = np.asarray(L0_attn_Wf, np.float32)
    g_w = np.asarray(L0_attn_g_w, np.float32)
    Wo = np.asarray(L0_attn_Wo, np.float32)
    wg = np.asarray(L0_ffn_w_gate, np.float32)
    wu = np.asarray(L0_ffn_w_up, np.float32)
    wd = np.asarray(L0_ffn_w_down, np.float32)

    nc = build_program()

    consts = np.zeros((128, 8), np.float32)
    consts[:, 0] = EPS
    consts[:, 1] = 1.0
    consts[:, 2] = 1e-12
    tri = _tri_mask(128, 128)
    tri_bf = np.concatenate([tri * 0.125, np.full((128, 128), 0.125, np.float32)],
                            axis=1).astype(ml_dtypes.bfloat16)
    ones_bf = np.ones((128, 128), ml_dtypes.bfloat16)
    ones_f = np.ones((128, 2), np.float32)
    masks = np.concatenate([tri, np.ones((128, 128), np.float32),
                            np.zeros((128, 128), np.float32), tri], axis=1)

    woT_full = np.ascontiguousarray(Wo.T)

    in_maps = []
    for c in range(NCORES):
        b, half = c // 2, c % 2
        cols = slice(half * 512, half * 512 + 512)
        wqkvfT = np.ascontiguousarray(np.concatenate(
            [Wq.T[:, cols], Wk.T[:, cols], Wv.T[:, cols], Wf.T[:, cols]], axis=1))
        def slice_pad(w, c):
            out = np.ones((NGU, D), np.float32)
            r0 = 3 * c * 128
            r1 = min(r0 + NGU, w.shape[0])
            if r0 < w.shape[0]:
                out[0:r1 - r0] = w[r0:r1]
            return out
        woT16 = np.zeros((2048, D), np.float32)
        gw16 = np.ones((2048,), np.float32)
        woT16[half * 512:(half + 1) * 512] = woT_full[0:512]
        woT16[1024 + half * 512:1024 + (half + 1) * 512] = woT_full[512:1024]
        gw16[half * 512:(half + 1) * 512] = g_w[0:512]
        gw16[1024 + half * 512:1024 + (half + 1) * 512] = g_w[512:1024]
        gwm = np.ascontiguousarray(gw16.reshape(16, 128).T)
        msk1 = np.zeros(16, np.float32)
        msk1[half * 4:(half + 1) * 4] = 1.0
        msk1[8 + half * 4:8 + (half + 1) * 4] = 1.0
        msk = np.tile(np.repeat(msk1, 2)[None, :], (128, 1)).astype(np.float32)
        in_maps.append({
            "x": x[b],
            "xh": np.ascontiguousarray(x[b, half * TH:(half + 1) * TH]),
            "wqkvfT": wqkvfT,
            "woT": woT16,
            "gw": gwm,
            "msk16": msk,
            "wg_s": slice_pad(wg, c),
            "wu_s": slice_pad(wu, c),
            "wd_s": np.ascontiguousarray(wd[c * 128:(c + 1) * 128]),
            "consts": consts,
            "tri_bf": tri_bf,
            "ones_bf": ones_bf,
            "ones_f": ones_f,
            "masks": masks,
        })

    res = run_bass_kernel_spmd(nc, in_maps, core_ids=list(range(NCORES)))
    if os.environ.get("KP_DBG", "0") == "1":
        np.save("/tmp/kdbg.npy", np.stack([res.results[c]["dbg"] for c in range(NCORES)]))
    out = np.empty((B, T, D), np.float32)
    for c in range(NCORES):
        b, half = c // 2, c % 2
        out[b, half * TH:(half + 1) * TH] = res.results[c]["out"]
    return out


if __name__ == "__main__":
    print("built ok:", build_program() is not None)


# version 07
# speedup vs baseline: 1.1729x; 1.1729x over previous
import sys
import os
sys.path.insert(0, "/opt/trn_rl_repo")
os.environ.setdefault("TRN_ALLOW_ICI_ALL_TO_ALL", "1")
import numpy as np
import ml_dtypes
import concourse.bass as bass
import concourse.mybir as mybir
from concourse.alu_op_type import AluOpType as OP
from concourse.bass_utils import run_bass_kernel_spmd
from contextlib import ExitStack

F32 = mybir.dt.float32
BF16 = mybir.dt.bfloat16
AF = mybir.ActivationFunctionType
AX = mybir.AxisListType

B, T, D = 4, 8192, 1024
H, DK, DV, M = 4, 256, 256, 256
I = 2816
NCORES = 8
C = 256
NCH = T // C
HPC = 2
TH = T // 2
NTT = TH // 128
EPS = 1e-5
MAGIC = 6291456.0
GUT = 3
NGU = GUT * 128
GUN = I // 128
ENGS = ("sync", "gpsimd", "vector", "scalar", "tensor")
SEMS = ["ld", "tr", "gs", "cc", "vs", "ss", "ps", "st"]


class Plan:
    def __init__(self):
        self.q = {e: [] for e in ENGS}
        self.cnt = {s: 0 for s in SEMS}

    def emit(self, eng, fn):
        self.q[eng].append(fn)

    def wait(self, eng, tok):
        if tok is None:
            return
        name, val = tok
        if val <= 0:
            return
        self.emit(eng, lambda e, S, name=name, val=val: e.wait_ge(S[name], val))

    def tok(self, name):
        return (name, self.cnt[name])

    def step(self, eng, body, makes=None, amt=1, waits=(), direct=False,
             default_inc=False):
        for t in waits:
            self.wait(eng, t)
        tokout = None
        if makes is not None:
            self.cnt[makes] += amt
            tokout = (makes, self.cnt[makes])
        use_drain = (makes is not None and amt != 16 and not direct
                     and eng in ("vector", "scalar"))

        pad_drain = False and (makes is None and eng in ("vector", "scalar"))

        def fn(e, S, body=body, makes=makes, amt=amt, use_drain=use_drain,
               pad_drain=pad_drain, default_inc=default_inc):
            instr = body(e)
            if makes is not None:
                if use_drain:
                    e.drain().then_inc(S[makes], amt)
                elif default_inc:
                    instr.then_inc(S[makes])
                else:
                    instr.then_inc(S[makes], amt)
            elif pad_drain:
                e.drain()
        self.emit(eng, fn)
        return tokout

    def spacer(self, eng):
        self.emit(eng, lambda e, S: e.drain())

    def barrier(self):
        toks = [self.tok(s) for s in SEMS]
        for e in ENGS:
            for t in toks:
                self.wait(e, t)


KP_NCH = int(os.environ.get("KP_NCH", str(T // C)))
KP_NTT = int(os.environ.get("KP_NTT", str(T // 256)))
KP_NT4 = int(os.environ.get("KP_NT4", os.environ.get("KP_NTT", str(T // 256))))
KP_FS = int(os.environ.get("KP_FS", "9"))
KP_DBG = os.environ.get("KP_DBG", "0") == "1"
KP_P0 = os.environ.get("KP_P0", "1") == "1"
KP_CC = os.environ.get("KP_CC", "1") == "1"


def build_program():
    nc = bass.Bass("TRN2", target_bir_lowering=False, debug=False)

    x_in = nc.dram_tensor("x", [T, D], F32, kind="ExternalInput").ap()
    xh_in = nc.dram_tensor("xh", [TH, D], F32, kind="ExternalInput").ap()
    wqkvfT = nc.dram_tensor("wqkvfT", [D, 2048], F32, kind="ExternalInput").ap()
    woT_in = nc.dram_tensor("woT", [2048, D], F32, kind="ExternalInput").ap()
    gw_in = nc.dram_tensor("gw", [128, 16], F32, kind="ExternalInput").ap()
    msk16_in = nc.dram_tensor("msk16", [128, 32], F32, kind="ExternalInput").ap()
    wg_s = nc.dram_tensor("wg_s", [NGU, D], F32, kind="ExternalInput").ap()
    wu_s = nc.dram_tensor("wu_s", [NGU, D], F32, kind="ExternalInput").ap()
    wd_s = nc.dram_tensor("wd_s", [128, I], F32, kind="ExternalInput").ap()
    consts = nc.dram_tensor("consts", [128, 8], F32, kind="ExternalInput").ap()
    tri_in = nc.dram_tensor("tri_bf", [128, 256], BF16, kind="ExternalInput").ap()
    ones_bf_in = nc.dram_tensor("ones_bf", [128, 128], BF16, kind="ExternalInput").ap()
    ones_f_in = nc.dram_tensor("ones_f", [128, 2], F32, kind="ExternalInput").ap()
    masks_in = nc.dram_tensor("masks", [128, 512], F32, kind="ExternalInput").ap()
    out_t = nc.dram_tensor("out", [TH, D], F32, kind="ExternalOutput").ap()
    dbg_t = nc.dram_tensor("dbg", [32, 128, 2848], F32,
                           kind="ExternalOutput").ap() if KP_DBG else None

    cc_wg_in = nc.dram_tensor("cc_wg_in", [NGU, D], BF16)
    cc_wg_out = nc.dram_tensor("cc_wg_out", [NCORES * NGU, D], BF16)
    cc_wu_in = nc.dram_tensor("cc_wu_in", [NGU, D], BF16)
    cc_wu_out = nc.dram_tensor("cc_wu_out", [NCORES * NGU, D], BF16)
    cc_wd_in = nc.dram_tensor("cc_wd_in", [128, I], BF16)
    cc_wd_out = nc.dram_tensor("cc_wd_out", [NCORES * 128, I], BF16)
    wgT_d = nc.dram_tensor("wgT_d", [128, 8 * I], BF16)
    wuT_d = nc.dram_tensor("wuT_d", [128, 8 * I], BF16)
    wdT_d = nc.dram_tensor("wdT_d", [128, GUN * D], BF16)
    cc_ov_in = nc.dram_tensor("cc_ov_in", [1024, TH], BF16)
    cc_ov_out = nc.dram_tensor("cc_ov_out", [2048, TH], BF16)
    wo16_d = nc.dram_tensor("wo16_d", [128, 16 * D], BF16)
    g_d = nc.dram_tensor("g_d", [TH, D], BF16)

    P = Plan()

    with ExitStack() as stack:
        def sb(name, shape, dt=F32):
            return stack.enter_context(nc.sbuf_tensor(name, shape, dt))

        cst = sb("cst", [128, 8])
        trib = sb("trib", [128, 256], BF16)
        onesb = sb("onesb", [128, 128], BF16)
        onesf = sb("onesf", [128, 2])
        msks = sb("msks", [128, 512])
        gwsb = sb("gwsb", [128, 16])
        msk16 = sb("s_msk16", [128, 32])
        wo_tmp = sb("wo_tmp", [128, D], BF16)
        lnm = sb("lnm", [128, 16])
        rmx = sb("rmx", [128, 8])
        dec2 = sb("dec2", [128, 2])
        eGrow = sb("eGrow", [128, 256], BF16)
        qX = sb("qX", [128, 512])
        qS = sb("qS", [128, 512])
        qC = sb("qC", [128, 512])
        qD = sb("qD", [128, 512])
        qE = sb("qE", [128, 512], BF16)
        qBM = sb("qBM", [128, 32])
        qSC = sb("qSC", [128, 32])
        qRS = sb("qRS", [128, 32])
        qLN = sb("qLN", [128, 32])
        qSB = sb("qSB", [128, 32], BF16)

        ARN = 46592
        arena = sb("arena", [128, ARN])

        def carve(off, ncols, dt=F32):
            ap = arena[:, off:off + ncols]
            return ap.bitcast(BF16) if dt == BF16 else ap

        def dbg(idx, ap, tokwait):
            if not KP_DBG:
                return
            n = ap.free_size()
            P.step("gpsimd", lambda e, idx=idx, ap=ap, n=n: e.dma_start(
                out=dbg_t[idx:idx + 1, :, 0:n], in_=ap),
                makes="gs", amt=16, waits=[tokwait])

        wproj = carve(0, 8192, BF16)
        osl = carve(8192, 16384, BF16)
        xb = carve(24576, 2048)
        xnb = carve(26624, 1024, BF16)
        xnT = carve(27648, 1024, BF16)
        fsb = carve(28672, 512)
        fab = carve(29184, 512)
        efb = carve(29696, 512)
        Eb = carve(30208, 512)
        Emb = carve(30720, 512)
        sgate = carve(31232, 512)
        okb = carve(31744, 512)
        o2b = carve(32256, 512)
        lnsq = carve(31744, 1024)
        lnsq2 = carve(28672, 2048)
        eGsb = carve(33280, 256)
        qT = carve(33536, 256, BF16)
        kT = carve(33792, 256, BF16)
        vtm = carve(34048, 256, BF16)
        ktime = carve(34304, 256, BF16)
        fhi = carve(34560, 256, BF16)
        fres = carve(34816, 256, BF16)
        stl = carve(35072, 256, BF16)
        shat = carve(35328, 256, BF16)
        stT = carve(35584, 256, BF16)
        qtl = carve(35840, 256, BF16)
        qtT = carve(36096, 256, BF16)
        Abf = carve(36352, 256, BF16)
        Bbf = carve(36608, 256, BF16)
        ovq = carve(36864, 256, BF16)
        S1 = [carve(37120, 512), carve(37632, 512)]
        S2 = [carve(38144, 512), carve(38656, 512)]
        S1b = [carve(39168, 256, BF16), carve(39424, 256, BF16)]
        S2b = [carve(39680, 256, BF16), carve(39936, 256, BF16)]

        wgT = carve(0, 11264, BF16)
        wuT = carve(11264, 11264, BF16)
        wdT = carve(22528, 11264, BF16)
        fx = carve(33792, 2048)
        fxn = carve(35840, 1024)
        fxq = carve(36864, 512, BF16)
        fxqT = carve(37376, 512, BF16)
        fxqT2 = carve(44544, 512, BF16)
        fsg = carve(37888, 512)
        fh = carve(38400, 2816)
        fhq = carve(41216, 1408, BF16)
        fhqT = carve(42624, 1408, BF16)
        fg = carve(44032, 512, BF16)
        wp_in = carve(40192, 2816)
        wp_bf = carve(43008, 1408, BF16)
        wp_tr = carve(44416, 1408, BF16)
        oT = carve(0, 16384, BF16)
        oT2 = carve(16384, 16384, BF16)
        wx = carve(32768, 1024)
        wgt = carve(33792, 512, BF16)
        wsf = carve(34304, 1024)
        wout = carve(35328, 1024)
        o2t = carve(36352, 2048)
        woTs = carve(38400, 8192, BF16)

        pq = stack.enter_context(nc.psum_tensor("pq", [128, 512], F32))
        pk = stack.enter_context(nc.psum_tensor("pk", [128, 512], F32))
        pv = stack.enter_context(nc.psum_tensor("pv", [128, 512], F32))
        pf = stack.enter_context(nc.psum_tensor("pf", [128, 512], F32))
        pl = stack.enter_context(nc.psum_tensor("pl", [128, 512], F32))
        pa = stack.enter_context(nc.psum_tensor("pa", [128, 512], F32))
        po = stack.enter_context(nc.psum_tensor("po", [128, 512], F32))
        pn = stack.enter_context(nc.psum_tensor("pn", [128, 512], F32))

        semh = {s: stack.enter_context(nc.semaphore(s)) for s in SEMS}
        block = stack.enter_context(nc.Block())

        def v3(ap, w=16):
            return ap.rearrange("p (b w) -> p b w", w=w)

        def emit_quant(x_ap, F, out_bf_ap, waits=()):
            nb = F // 16
            P.step("vector", lambda e: e.tensor_reduce(
                qBM[:, 0:nb], v3(x_ap), axis=AX.X, op=OP.max,
                apply_absolute_value=True),
                waits=list(waits) + [P.tok("ss")])
            tbm = P.step("vector", lambda e: e.nop(), makes="vs")
            P.step("scalar", lambda e: e.activation(
                qSC[:, 0:nb], qBM[:, 0:nb], AF.Identity,
                scale=0.16666666666666666, bias=cst[:, 2:3]), waits=[tbm])
            P.spacer("scalar")
            P.step("scalar", lambda e: e.activation(
                qLN[:, 0:nb], qSC[:, 0:nb], AF.Ln))
            P.spacer("scalar")
            tsc = P.step("scalar", lambda e: e.activation(
                qRS[:, 0:nb], qLN[:, 0:nb], AF.Exp, scale=-1.0), makes="ss")
            P.step("vector", lambda e: e.tensor_copy(qSB[:, 0:nb], qSC[:, 0:nb]),
                   waits=[tsc])
            tx = P.step("vector", lambda e: e.tensor_tensor(
                v3(qX[:, 0:F]), v3(x_ap),
                qRS[:, 0:nb].unsqueeze(2).broadcast_to([128, nb, 16]), OP.mult),
                makes="vs")
            tq = P.step("scalar", lambda e: e.activation(
                qS[:, 0:F], qX[:, 0:F], AF.Square), makes="ss", waits=[tx])
            P.step("vector", lambda e: e.tensor_scalar(
                qC[:, 0:F], qS[:, 0:F], 4.0, 1.0, OP.is_ge, OP.add), waits=[tq])
            P.step("vector", lambda e: e.tensor_scalar(
                qD[:, 0:F], qS[:, 0:F], 16.0, 1.0, OP.is_ge, OP.add))
            P.step("vector", lambda e: e.tensor_tensor(
                qS[:, 0:F], qC[:, 0:F], qD[:, 0:F], OP.mult))
            P.step("vector", lambda e: e.scalar_tensor_tensor(
                qC[:, 0:F], qS[:, 0:F], MAGIC, qX[:, 0:F], OP.mult, OP.add))
            P.step("vector", lambda e: e.scalar_tensor_tensor(
                qE[:, 0:F], qS[:, 0:F], -MAGIC, qC[:, 0:F], OP.mult, OP.add))
            P.spacer("vector")
            return P.step("vector", lambda e: e.tensor_tensor(
                v3(out_bf_ap), v3(qE[:, 0:F]),
                qSB[:, 0:nb].unsqueeze(2).broadcast_to([128, nb, 16]), OP.mult),
                makes="vs")

        def emit_ln2(x2_ap, out_aps, waits=(), scratch2=None):
            if scratch2 is None:
                scratch2 = lnsq2
            sc3 = scratch2.rearrange("p (s d2) -> p s d2", s=2)
            P.step("vector", lambda e: e.tensor_reduce(
                lnm[:, 0:2], x2_ap, axis=AX.X, op=OP.add), waits=waits)
            t0 = P.step("scalar", lambda e: e.activation(
                sc3, x2_ap, AF.Square), makes="ss", waits=waits)
            tsq = P.step("vector", lambda e: e.tensor_reduce(
                lnm[:, 2:4], sc3, axis=AX.X, op=OP.add), makes="vs", waits=[t0])
            P.step("vector", lambda e: e.tensor_scalar(
                lnm[:, 4:6], lnm[:, 0:2], 1.0 / D, None, OP.mult))
            P.step("vector", lambda e: e.tensor_scalar(
                lnm[:, 6:8], lnm[:, 2:4], 1.0 / D, None, OP.mult))
            P.spacer("vector")
            P.step("vector", lambda e: e.tensor_tensor(
                lnm[:, 8:10], lnm[:, 4:6], lnm[:, 4:6], OP.mult))
            P.spacer("vector")
            tvv = P.step("vector", lambda e: e.tensor_tensor(
                lnm[:, 10:12], lnm[:, 6:8], lnm[:, 8:10], OP.subtract), makes="vs")
            P.step("scalar", lambda e: e.activation(
                lnm[:, 10:12], lnm[:, 10:12], AF.Ln, bias=cst[:, 0:1]), waits=[tvv])
            P.spacer("scalar")
            tse = P.step("scalar", lambda e: e.activation(
                lnm[:, 12:14], lnm[:, 10:12], AF.Exp, scale=-0.5), makes="ss")
            tout = None
            for i, oa in enumerate(out_aps):
                tout = P.step("vector", lambda e, oa=oa, i=i: e.tensor_scalar(
                    oa, x2_ap[:, i, :],
                    lnm[:, 4 + i:5 + i], lnm[:, 12 + i:13 + i],
                    OP.subtract, OP.mult), makes="vs", waits=[tse] if i == 0 else ())
            return tout

        def quant_tile(src_dma_fn, F, dst_dma_fn, waits=()):
            tld = P.step("sync", src_dma_fn, makes="ld", amt=16, waits=waits)
            tq = None
            off = 0
            while off < F:
                w = min(512, F - off)
                tq = emit_quant(wp_in[:, off:off + w], w, wp_bf.bitcast(BF16) if False else wp_bf[:, off:off + w],
                                waits=[tld] if off == 0 else ())
                off += w
            return P.step("gpsimd", dst_dma_fn, makes="gs", amt=16, waits=[tq])

        for dst0, src0 in [(cst, consts), (trib, tri_in), (onesb, ones_bf_in),
                           (onesf, ones_f_in), (msks, masks_in)]:
            P.step("sync", lambda e, dst0=dst0, src0=src0: e.dma_start(
                out=dst0[:, :], in_=src0), makes="ld", amt=16)
        tcst = P.tok("ld")
        P.barrier()

        def p0_units():
            P.step("vector", lambda e: e.memset(wp_in[:, 0:D], 1.0))
            tpad = P.step("vector", lambda e: e.nop(), makes="vs")
            prev_store = None
            for (src, ccin) in [(wg_s, cc_wg_in), (wu_s, cc_wu_in)]:
                for t in range(GUT):
                    prev_store = quant_tile(
                        (lambda e, src=src, t=t: e.dma_start(
                            out=wp_in[:, 0:D], in_=src[t * 128:(t + 1) * 128, :])),
                        D,
                        (lambda e, ccin=ccin, t=t: e.dma_start(
                            out=ccin.ap()[t * 128:(t + 1) * 128, :], in_=wp_bf[:, 0:D])),
                        waits=[tpad, prev_store] if prev_store else [tpad])
                    yield
            prev_store = quant_tile(
                lambda e: e.dma_start(out=wp_in[:, 0:I], in_=wd_s), I,
                lambda e: e.dma_start(out=cc_wd_in.ap(), in_=wp_bf[:, 0:I]),
                waits=[prev_store])
            yield
            all8 = [list(range(NCORES))]
            P.step("gpsimd", lambda e: e.collective_compute(
                "AllGather", OP.bypass, replica_groups=all8,
                ins=[cc_wg_in.ap().opt()], outs=[cc_wg_out.ap().opt()]),
                makes="cc", waits=[prev_store], direct=True)
            P.step("gpsimd", lambda e: e.collective_compute(
                "AllGather", OP.bypass, replica_groups=all8,
                ins=[cc_wu_in.ap().opt()], outs=[cc_wu_out.ap().opt()]),
                makes="cc", direct=True)
            tccw = P.step("gpsimd", lambda e: e.collective_compute(
                "AllGather", OP.bypass, replica_groups=all8,
                ins=[cc_wd_in.ap().opt()], outs=[cc_wd_out.ap().opt()]),
                makes="cc", direct=True)
            yield
            tgw = P.tok("ld")
            prev = None
            for k in range(16):
                tldw = P.step("gpsimd", lambda e, k=k: e.dma_start(
                    out=wo_tmp[:, :], in_=woT_in[k * 128:(k + 1) * 128, :]),
                    makes="gs", amt=16, waits=[prev] if prev else [])
                tmul = P.step("vector", lambda e, k=k: e.tensor_scalar(
                    wo_tmp[:, :], wo_tmp[:, :], gwsb[:, k:k + 1], None,
                    OP.mult), makes="vs", waits=[tldw, tgw])
                prev = P.step("gpsimd", lambda e, k=k: e.dma_start(
                    out=wo16_d.ap()[:, k * D:(k + 1) * D], in_=wo_tmp[:, :]),
                    makes="gs", amt=16, waits=[tmul])
                if k % 4 == 3:
                    yield
            prev_tr = None
            prev_st = None
            nst = 0
            for (ccout, wT_d, ntile) in [(cc_wg_out, wgT_d, GUN), (cc_wu_out, wuT_d, GUN)]:
                for t in range(ntile):
                    r0 = (t // GUT) * NGU + (t % GUT) * 128
                    wl = [tccw] if t == 0 and ccout is cc_wg_out else []
                    if prev_tr is not None:
                        wl.append(prev_tr)
                    tld = P.step("gpsimd", lambda e, ccout=ccout, r0=r0: e.dma_start(
                        out=wp_bf[:, 0:D], in_=ccout.ap()[r0:r0 + 128, :]),
                        makes="gs", amt=16, waits=wl)
                    wl2 = [tld] + ([prev_st] if prev_st else [])
                    prev_tr = P.step("sync", lambda e: e.dma_start_transpose(
                        wp_tr[:, 0:1024].rearrange("p (b f) -> p b f", b=8),
                        wp_bf[:, 0:D]), makes="tr", amt=16, waits=wl2)
                    prev_st = P.step("gpsimd", lambda e, wT_d=wT_d, t=t: e.dma_start(
                        out=wT_d.ap().rearrange("p (k i) -> p k i", k=8)[:, :, t * 128:(t + 1) * 128],
                        in_=wp_tr[:, 0:1024].rearrange("p (b f) -> p b f", b=8)),
                        makes="gs", amt=16, waits=[prev_tr])
                    nst += 1
                    if nst % 2 == 0:
                        yield
            for k2 in range(8):
                tld = P.step("gpsimd", lambda e, k2=k2: e.dma_start(
                    out=wp_bf[:, 0:I], in_=cc_wd_out.ap()[k2 * 128:(k2 + 1) * 128, :]),
                    makes="gs", amt=16, waits=[prev_tr])
                prev_tr = P.step("sync", lambda e: e.dma_start_transpose(
                    wp_tr[:, 0:GUN * 128].rearrange("p (b f) -> p b f", b=GUN),
                    wp_bf[:, 0:I]), makes="tr", amt=16, waits=[tld, prev_st])
                prev_st = P.step("gpsimd", lambda e, k2=k2: e.dma_start(
                    out=wdT_d.ap().rearrange("p (j d2) -> p j d2", j=GUN)[:, :, k2 * 128:(k2 + 1) * 128],
                    in_=wp_tr[:, 0:GUN * 128].rearrange("p (b f) -> p b f", b=GUN)),
                    makes="gs", amt=16, waits=[prev_tr])
                if k2 % 2 == 1:
                    yield

        P.step("sync", lambda e: e.dma_start(out=gwsb[:, :], in_=gw_in),
               makes="ld", amt=16)
        P.step("sync", lambda e: e.dma_start(out=msk16[:, :], in_=msk16_in),
               makes="ld", amt=16)
        P.step("gpsimd", lambda e: e.dma_start(
            out=wproj.rearrange("p (k c) -> p k c", k=8),
            in_=wqkvfT.rearrange("(k p) c -> p k c", p=128)), makes="gs", amt=16)
        for hh in range(HPC):
            P.step("vector", lambda e, hh=hh: e.memset(S1[hh][:, :], 0.0))
            P.step("vector", lambda e, hh=hh: e.memset(S2[hh][:, :], 0.0))
            P.step("vector", lambda e, hh=hh: e.memset(S1b[hh][:, :], 0.0))
            P.step("vector", lambda e, hh=hh: e.memset(S2b[hh][:, :], 0.0))
        P.barrier()
        p0_iter = p0_units() if KP_P0 else iter(())

        def chunk_head(c, h):
            wp3 = wproj.rearrange("p (k c2) -> p k c2", k=8)
            xnT3 = xnT.rearrange("p (k t) -> p k t", k=8)
            colb = h * 256

            def proj_fm(pdst, mat):
                tk = None
                for mt in range(2):
                    for k in range(8):
                        col = mat * 512 + colb + mt * 128
                        tk = P.step("tensor", lambda e, k=k, col=col, mt=mt, pdst=pdst: e.matmul(
                            pdst[:, mt * 256:(mt + 1) * 256],
                            wp3[:, k, col:col + 128], xnT3[:, k, :],
                            start=(k == 0), stop=(k == 7)),
                            makes="ps" if k == 7 else None)
                return tk

            def proj_tm(pdst, mat):
                tk = None
                for tt in range(2):
                    for k in range(8):
                        col = mat * 512 + colb
                        tk = P.step("tensor", lambda e, k=k, col=col, tt=tt, pdst=pdst: e.matmul(
                            pdst[:, tt * 256:(tt + 1) * 256],
                            xnT3[:, k, tt * 128:(tt + 1) * 128],
                            wp3[:, k, col:col + 256],
                            start=(k == 0), stop=(k == 7)),
                            makes="ps" if k == 7 else None)
                return tk

            tq_ps = proj_fm(pq, 0)
            tk_ps = proj_fm(pk, 1)
            tv_ps = proj_tm(pv, 2)
            tf_ps = proj_tm(pf, 3)

            tqe = P.step("vector", lambda e: e.tensor_copy(ovq[:, :], pq[:, :]),
                         makes="vs", waits=[tq_ps])
            tqs = P.step("scalar", lambda e: e.activation(qT[:, :], ovq[:, :], AF.Silu),
                         makes="ss", waits=[tqe])
            tke = P.step("vector", lambda e: e.tensor_copy(ovq[:, :], pk[:, :]),
                         makes="vs", waits=[tk_ps, tqs])
            tks = P.step("scalar", lambda e: e.activation(kT[:, :], ovq[:, :], AF.Silu),
                         makes="ss", waits=[tke])
            tve = P.step("vector", lambda e: e.tensor_copy(ovq[:, :], pv[:, :]),
                         makes="vs", waits=[tv_ps, tks])
            tvs = P.step("scalar", lambda e: e.activation(vtm[:, :], ovq[:, :], AF.Silu),
                         makes="ss", waits=[tve])
            tf0 = P.step("scalar", lambda e: e.activation(fsb[:, :], pf[:, :], AF.Identity),
                         makes="ss", waits=[tf_ps])
            P.step("scalar", lambda e: e.activation(fab[:, :], fsb[:, :], AF.Abs))
            P.step("scalar", lambda e: e.activation(fab[:, :], fab[:, :], AF.Exp, scale=-1.0))
            tfl = P.step("scalar", lambda e: e.activation(fab[:, :], fab[:, :], AF.Ln,
                                                          bias=cst[:, 1:2]), makes="ss")
            tfr = P.step("vector", lambda e: e.scalar_tensor_tensor(
                fsb[:, :], fsb[:, :], 0.0, fab[:, :], OP.min, OP.subtract),
                makes="vs", waits=[tfl])
            P.step("vector", lambda e: e.tensor_copy(fhi[:, :], fsb[:, :]))
            P.spacer("vector")
            tfs = P.step("vector", lambda e: e.tensor_tensor(
                fres[:, :], fsb[:, :], fhi[:, :], OP.subtract), makes="vs")
            tef = P.step("scalar", lambda e: e.activation(efb[:, :], fsb[:, :], AF.Exp,
                                                          scale=0.125), makes="ss", waits=[tfr])
            tsg = P.step("vector", lambda e: e.tensor_scalar(
                sgate[:, :], efb[:, :], -1.0, 1.0, OP.mult, OP.add),
                makes="vs", waits=[tef])
            TRI = trib[:, 0:128]
            FUL = trib[:, 128:256]
            tlam = None
            for tt, seq in [(0, [(TRI, fhi, 0), (TRI, fres, 0)]),
                            (1, [(FUL, fhi, 0), (FUL, fres, 0), (TRI, fhi, 1), (TRI, fres, 1)])]:
                n = len(seq)
                for i2, (lt, rt, us) in enumerate(seq):
                    tlam = P.step("tensor", lambda e, lt=lt, rt=rt, us=us, tt=tt, i2=i2, n=n: e.matmul(
                        pl[:, tt * 256:(tt + 1) * 256], lt, rt[:, us * 256:(us + 1) * 256],
                        start=(i2 == 0), stop=(i2 == n - 1)),
                        makes="ps" if i2 == n - 1 else None,
                        waits=[tfs] if (tt == 0 and i2 == 0) else [])
            P.step("scalar", lambda e: e.activation(Eb[:, :], pl[:, :], AF.Exp),
                   waits=[tlam])
            tEm = P.step("scalar", lambda e: e.activation(Emb[:, :], pl[:, :], AF.Exp,
                                                          scale=-1.0), makes="ss")
            lseq = [(fhi, 0), (fres, 0), (fhi, 1), (fres, 1)]
            tlr = None
            for i2, (rt, us) in enumerate(lseq):
                tlr = P.step("tensor", lambda e, rt=rt, us=us, i2=i2: e.matmul(
                    pa[0:1, 0:256], onesb[:, 0:1], rt[:, us * 256:(us + 1) * 256],
                    start=(i2 == 0), stop=(i2 == 3)),
                    makes="ps" if i2 == 3 else None,
                    waits=[tfs] if i2 == 0 else [])
            tegf = P.step("scalar", lambda e: e.activation(
                eGsb[0:1, 0:256], pa[0:1, 0:256], AF.Exp, scale=0.125),
                makes="ss", waits=[tlr])
            tegb = P.step("vector", lambda e: e.tensor_copy(eGrow[0:1, 0:256],
                                                            eGsb[0:1, 0:256]),
                          makes="vs", waits=[tegf])
            tdc = None
            for ms in range(2):
                for i2, (rt, us) in enumerate(lseq):
                    tdc = P.step("tensor", lambda e, ms=ms, i2=i2, rt=rt, us=us: e.matmul(
                        pn[:, ms:ms + 1],
                        rt[:, us * 256 + ms * 128: us * 256 + ms * 128 + 128],
                        onesb[:, 0:1], start=(i2 == 0), stop=(i2 == 3)),
                        makes="ps" if i2 == 3 else None,
                        waits=[tfs] if (ms == 0 and i2 == 0) else [])
            tdc2 = P.step("scalar", lambda e: e.activation(dec2[:, 0:2], pn[:, 0:2],
                                                           AF.Exp, scale=0.125),
                          makes="ss", waits=[tdc])
            tge = P.step("tensor", lambda e: e.matmul(
                pn[:, 256:512], onesb[0:1, 0:128], eGrow[0:1, 0:256],
                start=True, stop=True), makes="ps", waits=[tegb, tdc2])
            teg = P.step("vector", lambda e: e.tensor_copy(eGsb[:, :], pn[:, 256:512]),
                         makes="vs", waits=[tge])
            tst = P.step("vector", lambda e: e.tensor_tensor(
                stl[:, :], sgate[:, :], Emb[:, :], OP.mult),
                makes="vs", waits=[tsg, tEm])
            P.spacer("vector")
            tsh = P.step("vector", lambda e: e.tensor_tensor(
                stl[:, :].rearrange("p (s m2) -> p s m2", s=2) if False else shat[:, :].rearrange("p (s m2) -> p s m2", s=2),
                stl[:, :].rearrange("p (s m2) -> p s m2", s=2),
                eGsb[:, :].unsqueeze(1).broadcast_to([128, 2, 256]), OP.mult),
                makes="vs", waits=[teg])
            tkt = None
            for j in range(2):
                tkt = P.step("sync", lambda e, j=j: e.dma_start_transpose(
                    ktime[:, :].rearrange("p (s d2) -> p s d2", s=2)[:, :, j * 128:(j + 1) * 128],
                    kT[:, j * 256:(j + 1) * 256]), makes="tr", amt=16,
                    waits=[tks] if j == 0 else [])
            tstT = None
            for ts2 in range(2):
                tstT = P.step("sync", lambda e, ts2=ts2: e.dma_start_transpose(
                    stT[:, :].rearrange("p (s t2) -> p s t2", s=2)[:, :, ts2 * 128:(ts2 + 1) * 128],
                    stl[:, ts2 * 256:(ts2 + 1) * 256]), makes="tr", amt=16,
                    waits=[tst] if ts2 == 0 else [])
            taq = None
            for ut in range(2):
                for j in range(2):
                    taq = P.step("tensor", lambda e, ut=ut, j=j: e.matmul(
                        pa[:, ut * 256:(ut + 1) * 256],
                        kT[:, j * 256 + ut * 128: j * 256 + ut * 128 + 128],
                        qT[:, j * 256:(j + 1) * 256],
                        start=(j == 0), stop=(j == 1)),
                        makes="ps" if j == 1 else None,
                        waits=[tqs, tks, tegf] if (ut == 0 and j == 0) else [])
            tA = None
            for ut in range(2):
                tA = P.step("vector", lambda e, ut=ut: e.tensor_tensor(
                    Abf[:, ut * 256:(ut + 1) * 256], msks[:, ut * 256:(ut + 1) * 256],
                    pa[:, ut * 256:(ut + 1) * 256], OP.mult),
                    makes="vs", waits=[taq] if ut == 0 else [])
            tokp = None
            for tt in range(2):
                mm = [("intra", j) for j in range(2)] + [("inter", j) for j in range(2)]
                for i2, (kind, j) in enumerate(mm):
                    if kind == "intra":
                        def body(e, j=j, tt=tt, i2=i2, n=len(mm)):
                            return e.matmul(
                                po[:, tt * 256:(tt + 1) * 256],
                                Abf[:, j * 256 + tt * 128: j * 256 + tt * 128 + 128],
                                stl[:, j * 256:(j + 1) * 256],
                                start=(i2 == 0), stop=(i2 == n - 1))
                    else:
                        def body(e, j=j, tt=tt, i2=i2, n=len(mm), h=h):
                            return e.matmul(
                                po[:, tt * 256:(tt + 1) * 256],
                                qT[:, j * 256 + tt * 128: j * 256 + tt * 128 + 128],
                                S1b[h][:, j * 256:(j + 1) * 256],
                                start=(i2 == 0), stop=(i2 == n - 1))
                    tokp = P.step("tensor", body,
                                  makes="ps" if i2 == len(mm) - 1 else None,
                                  waits=[tA, tst] if (tt == 0 and i2 == 0) else [])
            tokE = P.step("vector", lambda e: e.tensor_tensor(
                okb[:, :], Eb[:, :], po[:, :], OP.mult), makes="vs", waits=[tokp])
            tmx = P.step("vector", lambda e: e.tensor_reduce(
                rmx[:, 0:2], okb[:, :].rearrange("p (s m2) -> p s m2", s=2),
                axis=AX.X, op=OP.max, negate=True), makes="vs", waits=[tokE])
            tex0 = None
            for tt in range(2):
                tex0 = P.step("scalar", lambda e, tt=tt: e.activation(
                    o2b[:, tt * 256:(tt + 1) * 256], okb[:, tt * 256:(tt + 1) * 256],
                    AF.Exp, bias=rmx[:, tt:tt + 1]),
                    makes="ss", waits=[tmx] if tt == 0 else ())
            tex = P.step("vector", lambda e: e.tensor_reduce(
                rmx[:, 2:4], o2b[:, :].rearrange("p (s m2) -> p s m2", s=2),
                axis=AX.X, op=OP.add), makes="vs", waits=[tex0])
            P.step("scalar", lambda e: e.activation(
                rmx[:, 6:8], rmx[:, 2:4], AF.Ln), waits=[tex])
            P.spacer("scalar")
            trc = P.step("scalar", lambda e: e.activation(
                rmx[:, 4:6], rmx[:, 6:8], AF.Exp, scale=-1.0), makes="ss")
            tsm = None
            for tt in range(2):
                tsm = P.step("vector", lambda e, tt=tt: e.scalar_tensor_tensor(
                    qtl[:, tt * 256:(tt + 1) * 256], o2b[:, tt * 256:(tt + 1) * 256],
                    rmx[:, 4 + tt:5 + tt], Eb[:, tt * 256:(tt + 1) * 256],
                    OP.mult, OP.mult), makes="vs", waits=[trc] if tt == 0 else ())
            tqtT = None
            for ts2 in range(2):
                tqtT = P.step("sync", lambda e, ts2=ts2: e.dma_start_transpose(
                    qtT[:, :].rearrange("p (s t2) -> p s t2", s=2)[:, :, ts2 * 128:(ts2 + 1) * 128],
                    qtl[:, ts2 * 256:(ts2 + 1) * 256]), makes="tr", amt=16,
                    waits=[tsm] if ts2 == 0 else [])
            tbq = None
            for ut in range(2):
                for j in range(2):
                    tbq = P.step("tensor", lambda e, ut=ut, j=j: e.matmul(
                        pa[:, ut * 256:(ut + 1) * 256],
                        stT[:, j * 256 + ut * 128: j * 256 + ut * 128 + 128],
                        qtT[:, j * 256:(j + 1) * 256],
                        start=(j == 0), stop=(j == 1)),
                        makes="ps" if j == 1 else None,
                        waits=[tstT, tqtT, tA] if (ut == 0 and j == 0) else [])
            tB = None
            for ut in range(2):
                tB = P.step("vector", lambda e, ut=ut: e.tensor_tensor(
                    Bbf[:, ut * 256:(ut + 1) * 256], msks[:, ut * 256:(ut + 1) * 256],
                    pa[:, ut * 256:(ut + 1) * 256], OP.mult),
                    makes="vs", waits=[tbq] if ut == 0 else [])
            tov = None
            for vs2 in range(2):
                mm = [("intra", j) for j in range(2)] + [("inter", j) for j in range(2)]
                for i2, (kind, j) in enumerate(mm):
                    if kind == "intra":
                        def body(e, j=j, vs2=vs2, i2=i2, n=len(mm)):
                            return e.matmul(
                                po[:, vs2 * 256:(vs2 + 1) * 256],
                                vtm[:, j * 256 + vs2 * 128: j * 256 + vs2 * 128 + 128],
                                Bbf[:, j * 256:(j + 1) * 256],
                                start=(i2 == 0), stop=(i2 == n - 1))
                    else:
                        def body(e, j=j, vs2=vs2, i2=i2, n=len(mm), h=h):
                            return e.matmul(
                                po[:, vs2 * 256:(vs2 + 1) * 256],
                                S2b[h][:, j * 256 + vs2 * 128: j * 256 + vs2 * 128 + 128],
                                qtT[:, j * 256:(j + 1) * 256],
                                start=(i2 == 0), stop=(i2 == n - 1))
                    tov = P.step("tensor", body,
                                 makes="ps" if i2 == len(mm) - 1 else None,
                                 waits=[tB, tvs, tokE] if (vs2 == 0 and i2 == 0) else [])
            osl3 = osl.rearrange("p (f t) -> p f t", f=4)
            sl = osl3[:, 2 * h:2 * h + 2, c * C:(c + 1) * C]
            toe = P.step("vector", lambda e: e.tensor_copy(ovq[:, :], po[:, :]),
                         makes="vs", waits=[tov])
            tos = P.step("scalar", lambda e: e.activation(sl, ovq[:, :], AF.Silu),
                         makes="ss", waits=[toe])
            tn1 = None
            for ds in range(2):
                for j in range(2):
                    tn1 = P.step("tensor", lambda e, ds=ds, j=j: e.matmul(
                        pq[:, ds * 256:(ds + 1) * 256],
                        ktime[:, j * 256 + ds * 128: j * 256 + ds * 128 + 128],
                        shat[:, j * 256:(j + 1) * 256],
                        start=(j == 0), stop=(j == 1)),
                        makes="ps" if j == 1 else None,
                        waits=[tkt, tsh, tqe, tokp] if (ds == 0 and j == 0) else [])
            tn2 = None
            for ms in range(2):
                for j in range(2):
                    tn2 = P.step("tensor", lambda e, ms=ms, j=j: e.matmul(
                        pk[:, ms * 256:(ms + 1) * 256],
                        shat[:, j * 256 + ms * 128: j * 256 + ms * 128 + 128],
                        vtm[:, j * 256:(j + 1) * 256],
                        start=(j == 0), stop=(j == 1)),
                        makes="ps" if j == 1 else None,
                        waits=[tke, tov] if (ms == 0 and j == 0) else [])
            tS1m = P.step("vector", lambda e, h=h: e.tensor_tensor(
                fab[:, :].rearrange("p (s m2) -> p s m2", s=2),
                S1[h][:, :].rearrange("p (s m2) -> p s m2", s=2),
                eGsb[:, :].unsqueeze(1).broadcast_to([128, 2, 256]), OP.mult),
                makes="vs", waits=[tn1, teg])
            tS1 = P.step("vector", lambda e, h=h: e.tensor_tensor(
                S1[h][:, :], fab[:, :], pq[:, :], OP.add), makes="vs")
            tS2 = None
            for ms in range(2):
                tS2 = P.step("vector", lambda e, h=h, ms=ms: e.scalar_tensor_tensor(
                    S2[h][:, ms * 256:(ms + 1) * 256], S2[h][:, ms * 256:(ms + 1) * 256],
                    dec2[:, ms:ms + 1], pk[:, ms * 256:(ms + 1) * 256],
                    OP.mult, OP.add), makes="vs", waits=[tn2] if ms == 0 else [])
            P.step("vector", lambda e, h=h: e.tensor_copy(S1b[h][:, :], S1[h][:, :]),
                   waits=[tS1])
            P.step("vector", lambda e, h=h: e.tensor_copy(S2b[h][:, :], S2[h][:, :]),
                   waits=[tS2])
            tend = P.step("vector", lambda e: e.nop(), makes="vs")
            if c == 0 and h == 0:
                dbg(1, qT[:, :], tend)
                dbg(2, kT[:, :], tend)
                dbg(3, vtm[:, :], tend)
                dbg(4, fsb[:, :], tend)
                dbg(5, Eb[:, :], tend)
                dbg(6, Emb[:, :], tend)
                dbg(7, okb[:, :], tend)
                dbg(8, qtl[:, :], tend)
                dbg(9, Abf[:, :], tend)
                dbg(10, Bbf[:, :], tend)
                dbg(11, stl[:, :], tend)
                dbg(12, shat[:, :], tend)
                dbg(13, eGsb[:, :], tend)
                dbg(14, S1[0][:, :], tend)
                dbg(15, S2[0][:, :], tend)
                dbg(30, rmx[:, 0:8], tend)
                dbg(31, o2b[:, :], tend)

        for c in range(KP_NCH):
            next(p0_iter, None)
            next(p0_iter, None)
            P.step("sync", lambda e, c=c: e.dma_start(
                out=xb[:, 0:D], in_=x_in[c * C:c * C + 128, :]),
                makes="ld", amt=16)
            tx = P.step("sync", lambda e, c=c: e.dma_start(
                out=xb[:, D:2 * D], in_=x_in[c * C + 128:(c + 1) * C, :]),
                makes="ld", amt=16)
            tln = emit_ln2(xb.rearrange("p (s d2) -> p s d2", s=2),
                           [xnb[:, 0:D], xnb[:, D:2 * D]], waits=[tx])
            ttr = None
            for tt in range(2):
                ttr = P.step("sync", lambda e, tt=tt: e.dma_start_transpose(
                    xnT.rearrange("p (k t) -> p k t", k=8)[:, :, tt * 128:(tt + 1) * 128],
                    xnb[:, tt * D:(tt + 1) * D]), makes="tr", amt=16,
                    waits=[tln] if tt == 0 else [])
            if c == 0:
                dbg(0, xnb, P.tok("vs"))
                dbg(29, lnm[:, 0:16], P.tok("vs"))
                dbg(6, xb[:, 0:2048], P.tok("vs"))
            P.barrier()
            for h in range(HPC):
                chunk_head(c, h)
                P.barrier()
            if c == 0:
                dbg(16, osl.rearrange("p (f t) -> p f t", f=4)[:, :, 0:256],
                    P.tok("ss"))

        for _ in p0_iter:
            pass

        osl3x = osl.rearrange("p (f t) -> p f t", f=4)
        for jh in range(2):
            P.step("gpsimd", lambda e, jh=jh: e.dma_start(
                out=cc_ov_in.ap()[jh * 512:(jh + 1) * 512, :].rearrange(
                    "(f p) t -> p f t", p=128),
                in_=osl3x[:, :, jh * TH:(jh + 1) * TH]), makes="gs", amt=16)
        tst2 = P.tok("gs")
        pairs = [[0, 1], [2, 3], [4, 5], [6, 7]]
        if KP_CC:
            P.step("gpsimd", lambda e: e.collective_compute(
                "AllGather", OP.bypass, replica_groups=pairs,
                ins=[cc_ov_in.ap().opt()], outs=[cc_ov_out.ap().opt()]),
                makes="cc", waits=[tst2], direct=True)
        P.barrier()

        P.step("sync", lambda e: e.dma_start(out=wgT[:, :], in_=wgT_d.ap()),
               makes="ld", amt=16)
        P.step("sync", lambda e: e.dma_start(out=wuT[:, :], in_=wuT_d.ap()),
               makes="ld", amt=16)
        P.step("sync", lambda e: e.dma_start(out=wdT[:, :], in_=wdT_d.ap()),
               makes="ld", amt=16)
        P.barrier()
        wgT3 = wgT.rearrange("p (k i2) -> p k i2", k=8)
        wuT3 = wuT.rearrange("p (k i2) -> p k i2", k=8)
        wdT3 = wdT.rearrange("p (j d2) -> p j d2", j=GUN)
        fxqT3s = [fxqT.rearrange("p (k t) -> p k t", k=8),
                  fxqT2.rearrange("p (k t) -> p k t", k=8)]
        fhqT3 = fhqT.rearrange("p (j t) -> p j t", j=GUN)
        SECS = [(si * 512, min(512, I - si * 512)) for si in range(6)]

        fp = {"tln": None, "thq": None, "tqq": {}, "ttr": {}, "pe_gu": {},
              "prev_ge": None, "prev_he": None, "prev_th5": None,
              "tdn1": None, "tss2": [None, None], "tgd": None}

        def emit_xprep(tt):
            P.step("sync", lambda e, tt=tt: e.dma_start(
                out=fx[:, 0:D], in_=xh_in[tt * 128:(tt + 1) * 128, :]),
                makes="ld", amt=16,
                waits=[fp["tln"]] if fp["tln"] else [])
            tx = P.step("sync", lambda e, tt=tt: e.dma_start(
                out=fx[:, D:2 * D], in_=xh_in[tt * 128:(tt + 1) * 128, :]),
                makes="ld", amt=16)
            lw = [tx] + ([fp["thq"]] if fp["thq"] else [])
            tln = emit_ln2(fx.rearrange("p (s d2) -> p s d2", s=2),
                           [fxn[:, :]], waits=lw, scratch2=fh[:, 0:2 * D])
            tqq = emit_quant(fxn[:, 0:512], 512, fxq[:, 0:512], waits=[tln])
            tqq = emit_quant(fxn[:, 512:1024], 512, fxq[:, 512:1024], waits=[tqq])
            tw = [tqq] + ([fp["pe_gu"][tt - 2]] if (tt - 2) in fp["pe_gu"] else [])
            ttr = P.step("sync", lambda e, tt=tt: e.dma_start_transpose(
                fxqT3s[tt % 2], fxq[:, :]), makes="tr", amt=16, waits=tw)
            fp["tln"], fp["tqq"][tt], fp["ttr"][tt] = tln, tqq, ttr

        emit_xprep(0)
        for tt in range(KP_NTT):
            fxqT3 = fxqT3s[tt % 2]
            thq = None
            for (off, w) in SECS:
                tg = None
                for k in range(8):
                    if k == 0:
                        w0 = [fp["ttr"][tt]] if off == 0 else [fp["prev_ge"]]
                        if off == 0 and fp["prev_ge"] is not None:
                            w0.append(fp["prev_ge"])
                    else:
                        w0 = []
                    tg = P.step("tensor", lambda e, k=k, off=off, w=w, fxqT3=fxqT3: e.matmul(
                        pq[:, 0:w], fxqT3[:, k, :], wgT3[:, k, off:off + w],
                        start=(k == 0), stop=(k == 7)),
                        makes="ps" if k == 7 else None, waits=w0)
                tu = None
                for k in range(8):
                    w0 = [fp["prev_he"]] if (k == 0 and fp["prev_he"] is not None) else []
                    tu = P.step("tensor", lambda e, k=k, off=off, w=w, fxqT3=fxqT3: e.matmul(
                        pk[:, 0:w], fxqT3[:, k, :], wuT3[:, k, off:off + w],
                        start=(k == 0), stop=(k == 7)),
                        makes="ps" if k == 7 else None, waits=w0)
                sw = [tg] + ([fp["prev_he"]] if fp["prev_he"] is not None else [])
                tsg2 = P.step("scalar", lambda e, w=w: e.activation(
                    fsg[:, 0:w], pq[:, 0:w], AF.Silu), makes="ss", waits=sw)
                th = P.step("vector", lambda e, off=off, w=w: e.tensor_tensor(
                    fh[:, off:off + w], fsg[:, 0:w], pk[:, 0:w], OP.mult),
                    makes="vs", waits=[tsg2, tu])
                fp["prev_ge"], fp["prev_he"] = tsg2, th
                qw = [th] + ([fp["tdn1"]] if (off == 0 and fp["tdn1"]) else [])
                thq = emit_quant(fh[:, off:off + w], w,
                                 fhq[:, off:off + w], waits=qw)
            fp["thq"] = thq
            fp["pe_gu"][tt] = tu
            fp["prev_th5"] = th
            tw2 = [thq] + ([fp["tdn1"]] if fp["tdn1"] else [])
            ttr2 = P.step("sync", lambda e: e.dma_start_transpose(
                fhqT3, fhq[:, :]), makes="tr", amt=16, waits=tw2)
            if tt + 1 < KP_NTT:
                emit_xprep(tt + 1)
            tdn = []
            for half2 in range(2):
                td = None
                for j in range(GUN):
                    if j == 0:
                        w0 = [ttr2] + ([fp["tss2"][half2]] if fp["tss2"][half2] else [])
                    else:
                        w0 = []
                    td = P.step("tensor", lambda e, j=j, half2=half2: e.matmul(
                        (pv if half2 == 0 else pf)[:, 0:512],
                        fhqT3[:, j, :], wdT3[:, j, half2 * 512:(half2 + 1) * 512],
                        start=(j == 0), stop=(j == GUN - 1)),
                        makes="ps" if j == GUN - 1 else None,
                        waits=w0)
                tdn.append(td)
            fp["tdn1"] = tdn[1]
            tgo = None
            for half2 in range(2):
                sw2 = [tdn[half2]] + ([fp["prev_th5"]] if half2 == 0 else [tgo])
                tss2 = P.step("scalar", lambda e, half2=half2: e.activation(
                    fsg[:, 0:512], (pv if half2 == 0 else pf)[:, 0:512], AF.Silu),
                    makes="ss", waits=sw2)
                fp["tss2"][half2] = tss2
                gw2 = [tss2] + ([fp["tgd"]] if (half2 == 0 and fp["tgd"]) else [])
                tgo = P.step("vector", lambda e, half2=half2: e.tensor_copy(
                    fg[:, half2 * 512:(half2 + 1) * 512], fsg[:, 0:512]),
                    makes="vs", waits=gw2)
            tgd = P.step("gpsimd", lambda e, tt=tt: e.dma_start(
                out=g_d.ap()[tt * 128:(tt + 1) * 128, :], in_=fg[:, :]),
                makes="gs", amt=16, waits=[tgo])
            fp["tgd"] = tgd
            fp["prev_ge"] = tgo
        P.barrier()

        P.step("sync", lambda e: e.dma_start(
            out=oT.rearrange("p (k t) -> p k t", k=8),
            in_=cc_ov_out.ap()[0:1024, :].rearrange("(k p) t -> p k t", p=128)),
            makes="ld", amt=16)
        P.step("sync", lambda e: e.dma_start(
            out=oT2.rearrange("p (k t) -> p k t", k=8),
            in_=cc_ov_out.ap()[1024:2048, :].rearrange("(k p) t -> p k t", p=128)),
            makes="ld", amt=16)
        P.step("sync", lambda e: e.dma_start(
            out=woTs.rearrange("p (k d2) -> p k d2", k=16)[:, 0:8, :],
            in_=wo16_d.ap().rearrange("p (k d2) -> p k d2", k=16)[:, 0:8, :]),
            makes="ld", amt=16)
        P.step("sync", lambda e: e.dma_start(
            out=woTs.rearrange("p (k d2) -> p k d2", k=16)[:, 8:16, :],
            in_=wo16_d.ap().rearrange("p (k d2) -> p k d2", k=16)[:, 8:16, :]),
            makes="ld", amt=16)
        P.barrier()
        oT3a = oT.rearrange("p (k t) -> p k t", k=8)
        oT3b = oT2.rearrange("p (k t) -> p k t", k=8)
        woTs3 = woTs.rearrange("p (k d2) -> p k d2", k=16)
        o2t3 = o2t.rearrange("p (k t) -> p k t", k=16)
        laststore = None
        for tt in range(KP_NT4):
            txl = P.step("sync", lambda e, tt=tt: e.dma_start(
                out=wx[:, :], in_=xh_in[tt * 128:(tt + 1) * 128, :]),
                makes="ld", amt=16)
            tgl = P.step("gpsimd", lambda e, tt=tt: e.dma_start(
                out=wgt[:, :], in_=g_d.ap()[tt * 128:(tt + 1) * 128, :]),
                makes="gs", amt=16)
            tq2 = P.step("scalar", lambda e, tt=tt: e.activation(
                o2t3[:, 0:8, :], oT3a[:, :, tt * 128:(tt + 1) * 128], AF.Square),
                makes="ss")
            tq3 = P.step("scalar", lambda e, tt=tt: e.activation(
                o2t3[:, 8:16, :], oT3b[:, :, tt * 128:(tt + 1) * 128], AF.Square),
                makes="ss")
            tssq = None
            for k in range(16):
                tssq = P.step("tensor", lambda e, k=k: e.matmul(
                    pv[:, 0:2], o2t3[:, k, :], msk16[:, 2 * k:2 * k + 2],
                    start=(k == 0), stop=(k == 15)),
                    makes="ps" if k == 15 else None,
                    waits=[tq2, tq3] if k == 0 else [])
            tr1 = P.step("vector", lambda e: e.tensor_scalar(
                rmx[:, 0:2], pv[:, 0:2], 1.0 / (H * DV), None, OP.mult),
                makes="vs", waits=[tssq])
            P.step("scalar", lambda e: e.activation(
                rmx[:, 2:4], rmx[:, 0:2], AF.Ln, bias=cst[:, 0:1]), waits=[tr1])
            P.spacer("scalar")
            trt = P.step("scalar", lambda e: e.activation(
                rmx[:, 0:2], rmx[:, 2:4], AF.Exp, scale=-0.5), makes="ss")
            tmm = []
            for half2 in range(2):
                tm2 = None
                for k in range(16):
                    src = oT3a if k < 8 else oT3b
                    tm2 = P.step("tensor", lambda e, k=k, half2=half2, tt=tt, src=src: e.matmul(
                        (pq if half2 == 0 else pk)[:, 0:512],
                        src[:, k % 8, tt * 128:(tt + 1) * 128],
                        woTs3[:, k, half2 * 512:(half2 + 1) * 512],
                        start=(k == 0), stop=(k == 15)),
                        makes="ps" if k == 15 else None)
                tmm.append(tm2)
            tsf = None
            for half2 in range(2):
                tsf = P.step("vector", lambda e, half2=half2: e.tensor_scalar(
                    wsf[:, half2 * 512:(half2 + 1) * 512],
                    (pq if half2 == 0 else pk)[:, 0:512],
                    rmx[:, 0:1], None, OP.mult),
                    makes="vs", waits=[tmm[half2], trt] if half2 == 0 else [tmm[half2]])
            tw1 = P.step("vector", lambda e: e.tensor_tensor(
                wout[:, :], wsf[:, :], wgt[:, :], OP.mult),
                makes="vs", waits=[tsf, tgl])
            tw2 = P.step("vector", lambda e: e.tensor_tensor(
                wout[:, :], wout[:, :], wx[:, :], OP.add),
                makes="vs", waits=[txl])
            laststore = P.step("sync", lambda e, tt=tt: e.dma_start(
                out=out_t[tt * 128:(tt + 1) * 128, :], in_=wout[:, :]),
                makes="st", amt=16, waits=[tw2])
            if tt == 0:
                dbg(24, wsf[:, :], laststore)
                dbg(25, rmx[:, 0:8], laststore)
                dbg(26, oT.rearrange("p (k t) -> p k t", k=8)[:, :, 0:128], laststore)
                dbg(27, oT2.rearrange("p (k t) -> p k t", k=8)[:, :, 0:128], laststore)
                dbg(28, wgt[:, :], laststore)
            P.barrier()
        if laststore is not None:
            P.wait("sync", laststore)
            P.wait("gpsimd", P.tok("st"))

        @block.sync
        def _(sync):
            for fn in P.q["sync"]:
                fn(sync, semh)

        @block.gpsimd
        def _(g):
            for fn in P.q["gpsimd"]:
                fn(g, semh)

        @block.vector
        def _(v):
            for fn in P.q["vector"]:
                fn(v, semh)

        @block.scalar
        def _(s):
            for fn in P.q["scalar"]:
                fn(s, semh)

        @block.tensor
        def _(t):
            for fn in P.q["tensor"]:
                fn(t, semh)

    return nc


def _tri_mask(p, f):
    return (np.arange(p)[:, None] <= np.arange(f)[None, :]).astype(np.float32)


def kernel(L0_attn_hidden_states, L0_attn_Wq, L0_attn_Wk, L0_attn_Wv, L0_attn_Wf,
           L0_attn_g_w, L0_attn_Wo, L0_ffn_w_gate, L0_ffn_w_up, L0_ffn_w_down):
    x = np.ascontiguousarray(np.asarray(L0_attn_hidden_states, dtype=np.float32))
    Wq = np.asarray(L0_attn_Wq, np.float32)
    Wk = np.asarray(L0_attn_Wk, np.float32)
    Wv = np.asarray(L0_attn_Wv, np.float32)
    Wf = np.asarray(L0_attn_Wf, np.float32)
    g_w = np.asarray(L0_attn_g_w, np.float32)
    Wo = np.asarray(L0_attn_Wo, np.float32)
    wg = np.asarray(L0_ffn_w_gate, np.float32)
    wu = np.asarray(L0_ffn_w_up, np.float32)
    wd = np.asarray(L0_ffn_w_down, np.float32)

    nc = build_program()

    consts = np.zeros((128, 8), np.float32)
    consts[:, 0] = EPS
    consts[:, 1] = 1.0
    consts[:, 2] = 1e-12
    tri = _tri_mask(128, 128)
    tri_bf = np.concatenate([tri * 0.125, np.full((128, 128), 0.125, np.float32)],
                            axis=1).astype(ml_dtypes.bfloat16)
    ones_bf = np.ones((128, 128), ml_dtypes.bfloat16)
    ones_f = np.ones((128, 2), np.float32)
    masks = np.concatenate([tri, np.ones((128, 128), np.float32),
                            np.zeros((128, 128), np.float32), tri], axis=1)

    woT_full = np.ascontiguousarray(Wo.T)

    in_maps = []
    for c in range(NCORES):
        b, half = c // 2, c % 2
        cols = slice(half * 512, half * 512 + 512)
        wqkvfT = np.ascontiguousarray(np.concatenate(
            [Wq.T[:, cols], Wk.T[:, cols], Wv.T[:, cols], Wf.T[:, cols]], axis=1))
        def slice_pad(w, c):
            out = np.ones((NGU, D), np.float32)
            r0 = 3 * c * 128
            r1 = min(r0 + NGU, w.shape[0])
            if r0 < w.shape[0]:
                out[0:r1 - r0] = w[r0:r1]
            return out
        woT16 = np.zeros((2048, D), np.float32)
        gw16 = np.ones((2048,), np.float32)
        woT16[half * 512:(half + 1) * 512] = woT_full[0:512]
        woT16[1024 + half * 512:1024 + (half + 1) * 512] = woT_full[512:1024]
        gw16[half * 512:(half + 1) * 512] = g_w[0:512]
        gw16[1024 + half * 512:1024 + (half + 1) * 512] = g_w[512:1024]
        gwm = np.ascontiguousarray(gw16.reshape(16, 128).T)
        msk1 = np.zeros(16, np.float32)
        msk1[half * 4:(half + 1) * 4] = 1.0
        msk1[8 + half * 4:8 + (half + 1) * 4] = 1.0
        msk = np.tile(np.repeat(msk1, 2)[None, :], (128, 1)).astype(np.float32)
        in_maps.append({
            "x": x[b],
            "xh": np.ascontiguousarray(x[b, half * TH:(half + 1) * TH]),
            "wqkvfT": wqkvfT,
            "woT": woT16,
            "gw": gwm,
            "msk16": msk,
            "wg_s": slice_pad(wg, c),
            "wu_s": slice_pad(wu, c),
            "wd_s": np.ascontiguousarray(wd[c * 128:(c + 1) * 128]),
            "consts": consts,
            "tri_bf": tri_bf,
            "ones_bf": ones_bf,
            "ones_f": ones_f,
            "masks": masks,
        })

    res = run_bass_kernel_spmd(nc, in_maps, core_ids=list(range(NCORES)))
    if os.environ.get("KP_DBG", "0") == "1":
        np.save("/tmp/kdbg.npy", np.stack([res.results[c]["dbg"] for c in range(NCORES)]))
    out = np.empty((B, T, D), np.float32)
    for c in range(NCORES):
        b, half = c // 2, c % 2
        out[b, half * TH:(half + 1) * TH] = res.results[c]["out"]
    return out


if __name__ == "__main__":
    print("built ok:", build_program() is not None)
